# version 11
# baseline (speedup 1.0000x reference)
import math
import numpy as np
import ml_dtypes
import concourse.bass as bass
import concourse.mybir as mybir
from concourse.bass_utils import run_bass_kernel_spmd

F32 = mybir.dt.float32
BF16 = mybir.dt.bfloat16
AF = mybir.ActivationFunctionType
ALU = mybir.AluOpType
AX = mybir.AxisListType

REAL = dict(Bc=2, T=2048, D=2048, NH=16, NE=8, H=1024, HS=2048, TG=1024)
EXP_SHIFT = 16.0

CN_NEGRECD, CN_EPS, CN_RECD, CN_ISQH, CN_NEGSHIFT, CN_HALF, CN_NEG1 = range(7)
NCONST = 8


def chunks(total, step=512):
    return [(o, min(step, total - o)) for o in range(0, total, step)]


class Prog:
    CE = ("tensor", "vector", "scalar", "gpsimd", "sync")

    def __init__(self, nc, ctx):
        self.nc = nc
        self.ctx = ctx
        self.sems = {}
        self.thunks = {e: [] for e in self.CE}
        self.cnt = {}
        self.sat = {e: {} for e in self.CE}
        self.stream_engine = {}
        self.buf = {}
        for e in self.CE[:4]:
            self._sem("P_" + e)

    def _sem(self, name):
        if name not in self.sems:
            self.sems[name] = self.ctx.enter_context(self.nc.semaphore(name))
            self.cnt[name] = 0
        return self.sems[name]

    def _wait(self, e, tag):
        if tag is None:
            return
        sem, val = tag
        if sem == "P_tensor" and e == "tensor":
            return
        if self.sat[e].get(sem, 0) >= val:
            return
        self.sat[e][sem] = val
        h = self.sems[sem]
        self.thunks[e].append(lambda eng, h=h, val=val: eng.wait_ge(h, val))

    def _deps(self, e, reads, writes):
        for t in reads:
            st = self.buf.get(t)
            if st:
                self._wait(e, st["w"])
        for t in writes:
            st = self.buf.get(t)
            if st:
                self._wait(e, st["w"])
                for r in st["r"]:
                    self._wait(e, r)

    def _record(self, tag, reads, writes):
        for t in writes:
            self.buf[t] = {"w": tag, "r": []}
        for t in reads:
            self.buf.setdefault(t, {"w": None, "r": []})["r"].append(tag)

    def op(self, e, fn, reads=(), writes=(), dma=None):
        self._deps(e, reads, writes)
        if dma is None:
            sem, step = "P_" + e, 1
        else:
            sem, step = "D_" + dma, 16
            self._sem(sem)
            self.stream_engine[sem] = e
        self.cnt[sem] += step
        val = self.cnt[sem]
        h = self.sems[sem]
        self.thunks[e].append(lambda eng, fn=fn, h=h, step=step: fn(eng).then_inc(h, step))
        self._record((sem, val), reads, writes)

    def mm_group(self, fns, reads=(), writes=()):
        e = "tensor"
        self._deps(e, reads, writes)
        self.cnt["P_tensor"] += 1
        val = self.cnt["P_tensor"]
        h = self.sems["P_tensor"]
        for fn in fns[:-1]:
            self.thunks[e].append(lambda eng, fn=fn: fn(eng))
        self.thunks[e].append(lambda eng, fn=fns[-1], h=h: fn(eng).then_inc(h, 1))
        self._record(("P_tensor", val), reads, writes)

    def switch_region(self, old_tokens, new_tokens):
        tags = []
        for t in list(old_tokens) + list(new_tokens):
            st = self.buf.pop(t, None)
            if st:
                if st["w"]:
                    tags.append(st["w"])
                tags.extend(st["r"])
        for t in new_tokens:
            self.buf[t] = {"w": None, "r": list(tags)}

    def finalize(self):
        nc = self.nc
        for sname, ename in self.stream_engine.items():
            h, val = self.sems[sname], self.cnt[sname]
            if val > self.sat[ename].get(sname, 0):
                self.thunks[ename].append(lambda eng, h=h, val=val: eng.wait_ge(h, val))
        with nc.Block() as block:
            for ename in self.CE:
                ths = self.thunks[ename]

                def runner(eng, ths=ths):
                    for th in ths:
                        th(eng)
                getattr(block, ename)(runner)


def build_nc(cfg):
    Bc, T, D, NH, NE, H, HS, TG = (cfg[k] for k in ("Bc", "T", "D", "NH", "NE", "H", "HS", "TG"))
    assert D == NH * 128
    KC = D // 128
    MH = H // 128
    HS2 = HS // 2
    MSH = HS2 // 128
    TGT = TG // 128
    NG = (Bc * T) // TG
    NT = (Bc * T) // 128
    TH = T // 2
    QTH = TH // 128

    nc = bass.Bass("TRN2", target_bir_lowering=False)

    CW = min(512, D)
    NCC = D // CW
    x_in = nc.dram_tensor("x", [Bc * T, D], F32, kind="ExternalInput").ap()
    wqkvh = nc.dram_tensor("wqkvh", [NH * 3 * 128, KC * 128], F32, kind="ExternalInput").ap()
    wo_in = nc.dram_tensor("wo", [KC * 128, KC * 128], F32, kind="ExternalInput").ap()
    gateT = nc.dram_tensor("gateT", [D, 9], F32, kind="ExternalInput").ap()
    wgu = nc.dram_tensor("wgu", [NE * MH * 128, 2 * KC * 128], F32, kind="ExternalInput").ap()
    wdc = nc.dram_tensor("wdc", [NE * NCC * 128, MH * CW], F32, kind="ExternalInput").ap()
    shgu = nc.dram_tensor("shgu", [2 * MSH * 128, 2 * KC * 128], F32, kind="ExternalInput").ap()
    shdc = nc.dram_tensor("shdc", [2 * NCC * 128, MSH * CW], F32, kind="ExternalInput").ap()
    consts_in = nc.dram_tensor("consts", [128, NCONST], F32, kind="ExternalInput").ap()
    idf_in = nc.dram_tensor("idf", [128, 128], F32, kind="ExternalInput").ap()
    idb_in = nc.dram_tensor("idb", [128, 128], BF16, kind="ExternalInput").ap()
    mask_in = nc.dram_tensor("maskc", [128, 128], F32, kind="ExternalInput").ap()
    perm_in = nc.dram_tensor("permb", [128, 128], BF16, kind="ExternalInput").ap()
    ctbl_in = nc.dram_tensor("ctbl", [128, T], F32, kind="ExternalInput").ap()
    stbl_in = nc.dram_tensor("stbl", [128, T], F32, kind="ExternalInput").ap()
    out_t = nc.dram_tensor("out", [Bc * T, D], F32, kind="ExternalOutput").ap()
    xres = nc.dram_tensor("xres", [Bc * T, D], F32, kind="Internal").ap()
    aoh = nc.dram_tensor("aoh", [D, Bc * T], BF16, kind="Internal").ap()

    def alloc(name, shape, dtype):
        return nc.alloc_sbuf_tensor(name, shape, dtype)

    def at(name, shape, dtype, base, byte_off):
        off = nc.lookup_mloc(base).addr + byte_off
        return nc.alloc_sbuf_tensor_at(name, shape, dtype, offset=off).ap()

    UA_B = max(TGT * D * 4, KC * T * 2)
    UB_B = max(KC * TG * 2, KC * TH * 2)
    wdb_B = MH * min(512, D) * 2
    UC_ffn = MH * TG * 2 + 2 * wdb_B
    UC_att = (TH + T + T + TH + T + TH + TH) * 2
    UC_B = max(UC_ffn, UC_att)
    wbuf_B = KC * 128 * 2
    UD_B = 4 * wbuf_B

    UA = alloc("UA", [128, UA_B], mybir.dt.uint8)
    UB = alloc("UB", [128, UB_B], mybir.dt.uint8)
    UC = alloc("UC", [128, UC_B], mybir.dt.uint8)
    UD = alloc("UD", [128, UD_B], mybir.dt.uint8)

    acc = at("acc", [128, TGT * D], F32, UA, 0)
    xlnT_a = at("xlnT_a", [128, KC * T], BF16, UA, 0)
    xlnT_f = at("xlnT_f", [128, KC * TG], BF16, UB, 0)
    aoT = at("aoT", [128, KC * TH], BF16, UB, 0)
    h_buf = at("h_buf", [128, MH * TG], BF16, UC, 0)
    wd_b = [at(f"wd{i}", [128, MH * min(512, D)], BF16, UC, MH * TG * 2 + i * wdb_B)
            for i in range(2)]
    _o = 0
    qrotT = at("qrotT", [128, TH], BF16, UC, _o); _o += TH * 2
    krotT = at("krotT", [128, T], BF16, UC, _o); _o += T * 2
    v_sb = at("v_sb", [128, T], BF16, UC, _o); _o += T * 2
    q_sb = at("q_sb", [128, TH], BF16, UC, _o); _o += TH * 2
    k_sb = at("k_sb", [128, T], BF16, UC, _o); _o += T * 2
    wo_out = [at(f"wo_out{i}", [128, TH], BF16, UC, _o + i * TH * 2) for i in range(2)]
    wgu_b = [at(f"wgu{i}", [128, 2 * KC * 128], BF16, UD, i * 2 * wbuf_B) for i in range(2)]
    qkvw_b = [at(f"qkvw{i}", [128, KC * 128], BF16, UD, i * wbuf_B) for i in range(2)]
    wom_b = [at(f"wom{i}", [128, KC * 128], BF16, UD, (2 + i) * wbuf_B) for i in range(2)]

    x_sb = [alloc(f"x{i}", [128, D], F32).ap() for i in range(2)]
    sg_sb = [alloc(f"sg{i}", [128, 512], F32).ap() for i in range(2)]
    PMAX = max(D, T)
    p_sb = alloc("p", [128, 2 * PMAX], BF16).ap()
    pt_sb = alloc("pt", [128, max(D, T)], BF16).ap()
    ao_sb = alloc("ao_sb", [128, D], BF16).ap()
    rsc = alloc("rsc", [128, 1024], F32).ap()
    ctbl = alloc("ctbl_sb", [128, T], F32).ap()
    stbl = alloc("stbl_sb", [128, T], F32).ap()
    consts = alloc("consts_sb", [128, NCONST], F32).ap()
    idf = alloc("idf_sb", [128, 128], F32).ap()
    idb = alloc("idb_sb", [128, 128], BF16).ap()
    maskb = alloc("mask_sb", [128, 128], F32).ap()
    permb = alloc("perm_sb", [128, 128], BF16).ap()
    gate_sb = alloc("gate_sb", [128, KC * 9], BF16).ap()
    rout_s = alloc("rout_s", [128, TGT * 9], F32).ap()
    rout = alloc("rout", [128, TGT * 9], F32).ap()
    rt = alloc("rt", [128, 32], F32).ap()
    rtv = alloc("rtv", [128, 8], F32).ap()
    sml = alloc("sml", [128, 16], F32).ap()
    ssum = alloc("ssum", [128, 16], F32).ap()
    av_sb = alloc("av_sb", [128, 256], BF16).ap()
    scT = alloc("scT", [16, TG], F32).ap()

    ps = [nc.alloc_psum_tensor(f"ps{i}", [128, 512], F32).ap() for i in range(6)]
    psb = [nc.alloc_psum_tensor(f"psb{i}", [128, 1024], BF16).ap() for i in range(2)]

    from contextlib import ExitStack
    ctx = ExitStack()
    P = Prog(nc, ctx)

    c_ap = lambda i: consts[:, i:i + 1]

    for dst, src, tok in ((consts, consts_in, "consts"), (idf, idf_in, "idf"),
                          (idb, idb_in, "idb"), (maskb, mask_in, "maskb"),
                          (permb, perm_in, "permb"), (ctbl, ctbl_in, "ctbl"),
                          (stbl, stbl_in, "stbl")):
        P.op("sync", lambda e, d=dst, s=src: e.dma_start(d[:, :], s[:, :]), writes=[tok], dma=tok)
    P.op("gpsimd", lambda e: e.dma_start(gate_sb[:, :], gateT.rearrange("(kc p) e -> p kc e", p=128)),
         writes=["gate_sb"], dma="gate_sb")

    state = {"gq": 0, "wpar": 0, "dpar": 0}

    def emit_ln_stats(ti, sl, src_ap, src_tok, with_ao):
        xb = sl
        xt = f"x{xb}"
        sb = 8 * sl
        smt = f"sml{sl}"
        pb = sl * PMAX
        ptok = f"p{sl}"
        P.op("sync", lambda e: e.dma_start(x_sb[xb][:, :], src_ap[ti * 128:(ti + 1) * 128, :]),
             reads=[src_tok] if src_tok else (), writes=[xt], dma=xt)
        if with_ao:
            gtok = ti * 128
            reg = (gtok // T) * 2 + ((gtok % T) // TH)
            P.op("sync", lambda e: e.dma_start_transpose(ao_sb[:, :], aoh[:, ti * 128:(ti + 1) * 128]),
                 reads=[f"aohr{reg}"], writes=["ao_sb"], dma="ao_sb")
            P.op("vector", lambda e: e.tensor_add(x_sb[xb][:, :], x_sb[xb][:, :], ao_sb[:, :]),
                 reads=[xt, "ao_sb"], writes=[xt])
        P.op("scalar", lambda e, pb=pb: e.activation(p_sb[:, pb:pb + D], x_sb[xb][:, :], AF.Copy,
                                                     accum_out=sml[:, sb + 1:sb + 2]),
             reads=[xt], writes=[ptok, smt])
        P.op("scalar", lambda e, pb=pb: e.activation(p_sb[:, pb:pb + D], x_sb[xb][:, :], AF.Square,
                                                     accum_out=sml[:, sb + 0:sb + 1]),
             reads=[xt], writes=[ptok, smt])
        P.op("vector", lambda e: e.tensor_scalar(sml[:, sb + 3:sb + 4], sml[:, sb + 1:sb + 2],
                                                 sml[:, sb + 1:sb + 2], c_ap(CN_NEGRECD),
                                                 ALU.mult, ALU.mult),
             reads=[smt, "consts"], writes=[smt])
        P.op("vector", lambda e: e.tensor_scalar(sml[:, sb + 2:sb + 3], sml[:, sb + 1:sb + 2],
                                                 c_ap(CN_NEGRECD), None, ALU.mult),
             reads=[smt, "consts"], writes=[smt])
        P.op("vector", lambda e: e.tensor_add(sml[:, sb + 4:sb + 5], sml[:, sb + 3:sb + 4],
                                              sml[:, sb + 0:sb + 1]),
             reads=[smt], writes=[smt])
        P.op("scalar", lambda e: e.activation(sml[:, sb + 5:sb + 6], sml[:, sb + 4:sb + 5],
                                              AF.Sqrt, scale=c_ap(CN_RECD), bias=c_ap(CN_EPS)),
             reads=[smt, "consts"], writes=[smt])
        P.op("vector", lambda e: e.reciprocal(sml[:, sb + 6:sb + 7], sml[:, sb + 5:sb + 6]),
             reads=[smt], writes=[smt])
        P.op("vector", lambda e, pb=pb: e.tensor_scalar(p_sb[:, pb:pb + D], x_sb[xb][:, :],
                                                        sml[:, sb + 2:sb + 3], sml[:, sb + 6:sb + 7],
                                                        ALU.add, ALU.mult),
             reads=[xt, smt], writes=[ptok])

    def emit_ln_transpose(sl, xlnT_ap, xlnT_tok, tcol):
        pb = sl * PMAX
        ptok = f"p{sl}"
        fns = [lambda e, kc=kc, pb=pb: e.transpose(
            psb[kc // 8][:, (kc % 8) * 128:(kc % 8 + 1) * 128],
            p_sb[:, pb + kc * 128:pb + (kc + 1) * 128], idb[:, :]) for kc in range(KC)]
        P.mm_group(fns, reads=[ptok, "idb"], writes=["psb0", "psb1"] if KC > 8 else ["psb0"])
        n0 = min(KC, 8)
        d3 = xlnT_ap.rearrange("q (kc t) -> q kc t", kc=KC)[:, 0:n0, tcol:tcol + 128]
        s3 = psb[0].rearrange("q (kc t) -> q kc t", kc=8)[:, 0:n0, :]
        P.op("scalar", lambda e, d3=d3, s3=s3: e.copy(d3, s3), reads=["psb0"], writes=[xlnT_tok])
        if KC > 8:
            d3b = xlnT_ap.rearrange("q (kc t) -> q kc t", kc=KC)[:, 8:KC, tcol:tcol + 128]
            s3b = psb[1].rearrange("q (kc t) -> q kc t", kc=8)[:, 0:KC - 8, :]
            P.op("scalar", lambda e, d3b=d3b, s3b=s3b: e.copy(d3b, s3b), reads=["psb1"], writes=[xlnT_tok])

    def ln_group_steps(tis, src_ap, src_tok_fn, xlnT_ap, xlnT_tok, tcols, with_ao):
        prev = None
        for i, ti in enumerate(tis):
            sl = i % 2
            emit_ln_stats(ti, sl, src_ap, src_tok_fn(ti), with_ao)
            if prev is not None:
                emit_ln_transpose(prev[1], xlnT_ap, xlnT_tok, prev[0])
            prev = (tcols[i], sl)
            yield
        emit_ln_transpose(prev[1], xlnT_ap, xlnT_tok, prev[0])
        yield

    def emit_ln_group(tis, src_ap, src_tok_fn, xlnT_ap, xlnT_tok, tcols, with_ao):
        for _ in ln_group_steps(tis, src_ap, src_tok_fn, xlnT_ap, xlnT_tok, tcols, with_ao):
            pass

    def emit_ffn(src_ap, src_tok_fn, dst_ap, dst_tok_fn, with_ao):
        P.switch_region(["xlnT_a"], [f"acc{t}" for t in range(TGT)])
        P.switch_region(["aoT"], ["xlnT_f"])
        P.switch_region(["qrotT", "krotT", "v_sb", "q_sb", "k_sb", "wo_out0", "wo_out1"],
                        [f"h{m}" for m in range(MH)] + ["wd0", "wd1"])
        P.switch_region(["qkvw0", "qkvw1", "wom0", "wom1"], ["wgu0", "wgu1"])
        emit_ln_group([t for t in range(TGT)], src_ap, src_tok_fn, xlnT_f, "xlnT_f",
                      [t * 128 for t in range(TGT)], with_ao)
        for g in range(NG):
            t0 = g * TGT
            for noff, w in chunks(TG):
                fns = [lambda e, kc=kc, noff=noff, w=w: e.matmul(
                    ps[0][0:9, 0:w], gate_sb[:, kc * 9:(kc + 1) * 9],
                    xlnT_f[:, kc * TG + noff:kc * TG + noff + w],
                    start=(kc == 0), stop=(kc == KC - 1)) for kc in range(KC)]
                P.mm_group(fns, reads=["gate_sb", "xlnT_f"], writes=["ps0"])
                P.op("scalar", lambda e, noff=noff, w=w: e.copy(
                    scT[0:9, noff:noff + w], ps[0][0:9, 0:w]), reads=["ps0"], writes=["scT"])
            for t in range(TGT):
                P.mm_group([lambda e, t=t: e.transpose(ps[1][:, 0:9],
                                                       scT[0:9, t * 128:(t + 1) * 128],
                                                       idf[0:9, 0:9])],
                           reads=["scT", "idf"], writes=["ps1"])
                P.op("scalar", lambda e, t=t: e.copy(rout_s[:, t * 9:(t + 1) * 9], ps[1][:, 0:9]),
                     reads=["ps1"], writes=["rout_s"])
            for t in range(TGT):
                s8 = rout_s[:, t * 9:t * 9 + 8]
                P.op("vector", lambda e, s8=s8: e.reduce_max(rtv[:, 0:1], s8, axis=AX.X),
                     reads=["rout_s"], writes=["rtv"])
                P.op("vector", lambda e, s8=s8: e.tensor_scalar(rt[:, 0:8], s8, rtv[:, 0:1], None, ALU.is_ge),
                     reads=["rout_s", "rtv"], writes=["rt"])
                P.op("vector", lambda e, s8=s8: e.scalar_tensor_tensor(rt[:, 8:16], rt[:, 0:8],
                                                                       -1e30, s8, ALU.mult, ALU.add),
                     reads=["rt", "rout_s"], writes=["rt"])
                P.op("vector", lambda e: e.reduce_max(rtv[:, 1:2], rt[:, 8:16], axis=AX.X),
                     reads=["rt"], writes=["rtv"])
                P.op("vector", lambda e: e.tensor_sub(rtv[:, 2:3], rtv[:, 0:1], rtv[:, 1:2]),
                     reads=["rtv"], writes=["rtv"])
                P.op("scalar", lambda e: e.activation(rtv[:, 3:4], rtv[:, 2:3], AF.Sigmoid),
                     reads=["rtv"], writes=["rtv"])
                P.op("scalar", lambda e: e.activation(rtv[:, 4:5], rtv[:, 2:3], AF.Sigmoid,
                                                      scale=c_ap(CN_NEG1)),
                     reads=["rtv", "consts"], writes=["rtv"])
                P.op("vector", lambda e: e.tensor_scalar(rt[:, 16:24], rt[:, 8:16], rtv[:, 1:2],
                                                         None, ALU.is_ge),
                     reads=["rt", "rtv"], writes=["rt"])
                P.op("vector", lambda e: e.tensor_scalar(rt[:, 24:32], rt[:, 16:24], rtv[:, 4:5],
                                                         None, ALU.mult),
                     reads=["rt", "rtv"], writes=["rt"])
                P.op("vector", lambda e, t=t: e.scalar_tensor_tensor(rout[:, t * 9:t * 9 + 8],
                                                                     rt[:, 0:8], rtv[:, 3:4],
                                                                     rt[:, 24:32], ALU.mult, ALU.add),
                     reads=["rt", "rtv"], writes=["rout"])
                P.op("scalar", lambda e, t=t: e.activation(rout[:, t * 9 + 8:t * 9 + 9],
                                                           rout_s[:, t * 9 + 8:t * 9 + 9], AF.Sigmoid),
                     reads=["rout_s"], writes=["rout"])

            def a_phase(wsrc, mh, chunk0):
                for m in range(mh):
                    wp = state["wpar"] % 2
                    state["wpar"] += 1
                    gb = wgu_b[wp]
                    gt = f"wgu{wp}"
                    r0 = (chunk0 + m) * 128
                    P.op("gpsimd", lambda e, gb=gb, r0=r0, wsrc=wsrc: e.dma_start(
                        gb[:, :], wsrc[r0:r0 + 128, :]), writes=[gt], dma=gt)
                    for ci, (noff, w) in enumerate(chunks(TG)):
                        par = (m * len(chunks(TG)) + ci) % 3
                        pg, pu = 2 * par, 2 * par + 1
                        fns = [lambda e, kc=kc, pg=pg, gb=gb, noff=noff, w=w: e.matmul(
                            ps[pg][:, 0:w], gb[:, kc * 128:(kc + 1) * 128],
                            xlnT_f[:, kc * TG + noff:kc * TG + noff + w],
                            start=(kc == 0), stop=(kc == KC - 1)) for kc in range(KC)]
                        P.mm_group(fns, reads=[gt, "xlnT_f"], writes=[f"ps{pg}"])
                        fns = [lambda e, kc=kc, pu=pu, gb=gb, noff=noff, w=w: e.matmul(
                            ps[pu][:, 0:w], gb[:, (KC + kc) * 128:(KC + kc + 1) * 128],
                            xlnT_f[:, kc * TG + noff:kc * TG + noff + w],
                            start=(kc == 0), stop=(kc == KC - 1)) for kc in range(KC)]
                        P.mm_group(fns, reads=[gt, "xlnT_f"], writes=[f"ps{pu}"])
                        sgp = par % 2
                        P.op("scalar", lambda e, sgp=sgp, pg=pg, w=w: e.activation(
                            sg_sb[sgp][:, 0:w], ps[pg][:, 0:w], AF.Silu),
                            reads=[f"ps{pg}"], writes=[f"sg{sgp}"])
                        P.op("vector", lambda e, sgp=sgp, pu=pu, m=m, noff=noff, w=w: e.tensor_mul(
                            h_buf[:, m * TG + noff:m * TG + noff + w], sg_sb[sgp][:, 0:w], ps[pu][:, 0:w]),
                            reads=[f"sg{sgp}", f"ps{pu}"], writes=[f"h{m}"])

            def b_phase(wsrc_d, mh, chunk0, scol_fn, first_write, filler=None):
                for cq_i, (cqoff, w) in enumerate(chunks(D)):
                    if filler is not None:
                        filler()
                    dp = state["dpar"] % 2
                    state["dpar"] += 1
                    db, dt_ = wd_b[dp], f"wd{dp}"
                    r0 = (chunk0 + cq_i) * 128
                    P.op("gpsimd", lambda e, db=db, r0=r0, mh=mh, w=w, wsrc_d=wsrc_d: e.dma_start(
                        db[:, 0:mh * w], wsrc_d[r0:r0 + 128, 0:mh * w]), writes=[dt_], dma=dt_)
                    for t in range(TGT):
                        pp = 2 + ((cq_i * TGT + t) % 4)
                        fns = [lambda e, m=m, pp=pp, db=db, t=t, w=w: e.matmul(
                            ps[pp][:, 0:w], h_buf[:, m * TG + t * 128:m * TG + (t + 1) * 128],
                            db[:, m * w:(m + 1) * w],
                            start=(m == 0), stop=(m == mh - 1)) for m in range(mh)]
                        P.mm_group(fns, reads=[f"h{m}" for m in range(mh)] + [dt_], writes=[f"ps{pp}"])
                        scol = scol_fn(t)
                        a_sl = acc[:, t * D + cqoff:t * D + cqoff + w]
                        if first_write:
                            P.op("vector", lambda e, pp=pp, scol=scol, a_sl=a_sl, w=w: e.tensor_scalar(
                                a_sl, ps[pp][:, 0:w], scol, None, ALU.mult),
                                reads=[f"ps{pp}", "rout"], writes=[f"acc{t}"])
                        else:
                            P.op("vector", lambda e, pp=pp, scol=scol, a_sl=a_sl, w=w: e.scalar_tensor_tensor(
                                a_sl, ps[pp][:, 0:w], scol, a_sl, ALU.mult, ALU.add),
                                reads=[f"ps{pp}", "rout", f"acc{t}"], writes=[f"acc{t}"])

            for ex in range(NE):
                a_phase(wgu, MH, ex * MH)
                b_phase(wdc, MH, ex * NCC, lambda t, ex=ex: rout[:, t * 9 + ex:t * 9 + ex + 1],
                        first_write=(ex == 0))
            ln_gen = None
            if g + 1 < NG:
                nt0 = (g + 1) * TGT
                ln_gen = ln_group_steps([nt0 + t for t in range(TGT)], src_ap, src_tok_fn,
                                        xlnT_f, "xlnT_f", [t * 128 for t in range(TGT)], with_ao)

            def filler(k=(2 * TGT + 2) // max(1, 2 * NCC) + 1):
                if ln_gen is not None:
                    for _ in range(k):
                        if next(ln_gen, "end") == "end":
                            break
            for hf in range(2):
                a_phase(shgu, MSH, hf * MSH)
                b_phase(shdc, MSH, hf * NCC, lambda t: rout[:, t * 9 + 8:t * 9 + 9],
                        first_write=False, filler=filler if hf == 1 else None)
            if ln_gen is not None:
                for _ in ln_gen:
                    pass

            for t in range(TGT):
                ti = t0 + t
                xb = t % 2
                xt = f"x{xb}"
                P.op("sync", lambda e, ti=ti, xb=xb: e.dma_start(
                    x_sb[xb][:, :], src_ap[ti * 128:(ti + 1) * 128, :]),
                    reads=[src_tok_fn(ti)] if src_tok_fn(ti) else (), writes=[xt], dma=xt)
                if with_ao:
                    reg = ((ti * 128) // T) * 2 + (((ti * 128) % T) // TH)
                    P.op("sync", lambda e, ti=ti: e.dma_start_transpose(
                        ao_sb[:, :], aoh[:, ti * 128:(ti + 1) * 128]),
                        reads=[f"aohr{reg}"], writes=["ao_sb"], dma="ao_sb")
                P.op("vector", lambda e, t=t, xb=xb: e.scalar_tensor_tensor(
                    x_sb[xb][:, :], acc[:, t * D:(t + 1) * D], c_ap(CN_HALF), x_sb[xb][:, :],
                    ALU.mult, ALU.add),
                    reads=[f"acc{t}", "consts", xt], writes=[xt])
                if with_ao:
                    P.op("vector", lambda e, xb=xb: e.tensor_add(x_sb[xb][:, :], x_sb[xb][:, :], ao_sb[:, :]),
                         reads=[xt, "ao_sb"], writes=[xt])
                P.op("sync", lambda e, ti=ti, xb=xb: e.dma_start(
                    dst_ap[ti * 128:(ti + 1) * 128, :], x_sb[xb][:, :]),
                    reads=[xt], writes=[dst_tok_fn(ti)], dma=xt)

    def emit_attn():
        P.switch_region([f"acc{t}" for t in range(TGT)], ["xlnT_a"])
        P.switch_region(["xlnT_f"], ["aoT"])
        P.switch_region([f"h{m}" for m in range(MH)] + ["wd0", "wd1"],
                        ["qrotT", "krotT", "v_sb", "q_sb", "k_sb", "wo_out0", "wo_out1"])
        P.switch_region(["wgu0", "wgu1"], ["qkvw0", "qkvw1", "wom0", "wom1"])
        pcyc = [0]

        def nps():
            i = pcyc[0] % 4
            pcyc[0] += 1
            return i

        for b in range(Bc):
            bt0 = b * (T // 128)
            emit_ln_group([bt0 + tl for tl in range(T // 128)], xres, lambda ti: f"xr{ti}",
                          xlnT_a, "xlnT_a", [tl * 128 for tl in range(T // 128)], False)
            for hf in range(2):
                qoff0 = hf * TH
                kvlen = (hf + 1) * TH
                for h in range(NH):
                    plan = [(0, TH, qoff0, q_sb, "q_sb"), (1, kvlen, 0, k_sb, "k_sb")]
                    pchunks = {}
                    for which, length, poff, stage, stok in plan:
                        wp = state["wpar"] % 2
                        state["wpar"] += 1
                        wb, wt = qkvw_b[wp], f"qkvw{wp}"
                        rows = (h * 3 + which) * 128
                        P.op("gpsimd", lambda e, wb=wb, rows=rows: e.dma_start(
                            wb[:, :], wqkvh[rows:rows + 128, :]), writes=[wt], dma=wt)
                        lst = []
                        for noff, w in chunks(length):
                            pi = nps()
                            fns = [lambda e, kc=kc, pi=pi, wb=wb, noff=noff, w=w, poff=poff: e.matmul(
                                ps[pi][:, 0:w], wb[:, kc * 128:(kc + 1) * 128],
                                xlnT_a[:, kc * T + poff + noff:kc * T + poff + noff + w],
                                start=(kc == 0), stop=(kc == KC - 1)) for kc in range(KC)]
                            P.mm_group(fns, reads=[wt, "xlnT_a"], writes=[f"ps{pi}"])
                            P.op("scalar", lambda e, pi=pi, noff=noff, w=w, sb=stage: e.copy(
                                sb[:, noff:noff + w], ps[pi][:, 0:w]), reads=[f"ps{pi}"], writes=[stok])
                            lst.append((noff, w, pi))
                        pchunks[which] = lst
                    for which, dst, dst_tok, stage, stok, poff in (
                            (0, qrotT, "qrotT", q_sb, "q_sb", qoff0), (1, krotT, "krotT", k_sb, "k_sb", 0)):
                        for noff, w, _ in pchunks[which]:
                            pi = nps()
                            P.mm_group([lambda e, pi=pi, noff=noff, w=w, sb=stage: e.matmul(
                                ps[pi][:, 0:w], permb[:, :], sb[:, noff:noff + w], start=True, stop=True)],
                                reads=["permb", stok], writes=[f"ps{pi}"])
                            P.op("vector", lambda e, noff=noff, w=w, sb=stage, poff=poff: e.tensor_mul(
                                rsc[:, 0:w], sb[:, noff:noff + w], ctbl[:, poff + noff:poff + noff + w]),
                                reads=[stok, "ctbl"], writes=["rsc0"])
                            P.op("vector", lambda e, pi=pi, noff=noff, w=w, poff=poff: e.tensor_mul(
                                rsc[:, 512:512 + w], ps[pi][:, 0:w], stbl[:, poff + noff:poff + noff + w]),
                                reads=[f"ps{pi}", "stbl"], writes=["rsc1"])
                            P.op("vector", lambda e, noff=noff, w=w, dst=dst: e.tensor_add(
                                dst[:, noff:noff + w], rsc[:, 0:w], rsc[:, 512:512 + w]),
                                reads=["rsc0", "rsc1"], writes=[dst_tok])
                    wp = state["wpar"] % 2
                    state["wpar"] += 1
                    wb, wt = qkvw_b[wp], f"qkvw{wp}"
                    rows = (h * 3 + 2) * 128
                    P.op("gpsimd", lambda e, wb=wb, rows=rows: e.dma_start(
                        wb[:, :], wqkvh[rows:rows + 128, :]), writes=[wt], dma=wt)
                    for noff, w in chunks(kvlen):
                        pi = nps()
                        fns = [lambda e, kc=kc, pi=pi, wb=wb, noff=noff, w=w: e.matmul(
                            ps[pi][:, 0:w], wb[:, kc * 128:(kc + 1) * 128],
                            xlnT_a[:, kc * T + noff:kc * T + noff + w],
                            start=(kc == 0), stop=(kc == KC - 1)) for kc in range(KC)]
                        P.mm_group(fns, reads=[wt, "xlnT_a"], writes=[f"ps{pi}"])
                        P.op("scalar", lambda e, pi=pi, noff=noff, w=w: e.copy(
                            pt_sb[:, noff:noff + w], ps[pi][:, 0:w]), reads=[f"ps{pi}"], writes=["pt"])
                    nvc = kvlen // 128
                    fns = [lambda e, kc=kc: e.transpose(
                        psb[kc // 8][:, (kc % 8) * 128:(kc % 8 + 1) * 128],
                        pt_sb[:, kc * 128:(kc + 1) * 128], idb[:, :]) for kc in range(nvc)]
                    P.mm_group(fns, reads=["pt", "idb"],
                               writes=["psb0", "psb1"] if nvc > 8 else ["psb0"])
                    n0 = min(nvc, 8)
                    P.op("scalar", lambda e, n0=n0: e.copy(v_sb[:, 0:n0 * 128], psb[0][:, 0:n0 * 128]),
                         reads=["psb0"], writes=["v_sb"])
                    if nvc > 8:
                        n1 = nvc - 8
                        P.op("scalar", lambda e, n1=n1: e.copy(
                            v_sb[:, 1024:1024 + n1 * 128], psb[1][:, 0:n1 * 128]),
                            reads=["psb1"], writes=["v_sb"])

                    def s1bank(ql, ci):
                        return (ci + 2 * (ql % 2)) % 4

                    def stage1mm(ql):
                        qi = hf * QTH + ql
                        L = (qi + 1) * 128
                        scs = chunks(L)
                        for ci, (noff, w) in enumerate(scs):
                            pc = s1bank(ql, ci)
                            P.mm_group([lambda e, pc=pc, noff=noff, w=w, ql=ql: e.matmul(
                                ps[pc][:, 0:w], qrotT[:, ql * 128:(ql + 1) * 128],
                                krotT[:, noff:noff + w], start=True, stop=True)],
                                reads=["qrotT", "krotT"], writes=[f"ps{pc}"])

                    def stage1post(ql):
                        qi = hf * QTH + ql
                        L = (qi + 1) * 128
                        scs = chunks(L)
                        sl = ql % 2
                        pb = sl * PMAX
                        sb8 = sl * 8
                        dci, doff = s1bank(ql, (L - 128) // 512), (L - 128) % 512
                        P.op("vector", lambda e, dci=dci, doff=doff: e.tensor_add(
                            ps[dci][:, doff:doff + 128], ps[dci][:, doff:doff + 128], maskb[:, :]),
                            reads=[f"ps{dci}", "maskb"], writes=[f"ps{dci}"])
                        for ci, (noff, w) in enumerate(scs):
                            pc = s1bank(ql, ci)
                            P.op("scalar", lambda e, pc=pc, ci=ci, noff=noff, w=w, pb=pb, sb8=sb8: e.activation(
                                p_sb[:, pb + noff:pb + noff + w], ps[pc][:, 0:w], AF.Exp,
                                scale=c_ap(CN_ISQH), bias=c_ap(CN_NEGSHIFT),
                                accum_out=ssum[:, sb8 + ci:sb8 + ci + 1]),
                                reads=[f"ps{pc}", "consts"], writes=[f"p{sl}", "ssum"])
                        ncs = len(scs)
                        if ncs > 1:
                            P.op("vector", lambda e, ncs=ncs, sb8=sb8: e.reduce_sum(
                                ssum[:, sb8 + 4:sb8 + 5], ssum[:, sb8:sb8 + ncs], axis=AX.X),
                                reads=["ssum"], writes=["ssum"])
                            scol = 4
                        else:
                            scol = 0
                        P.op("vector", lambda e, scol=scol, sb8=sb8: e.reciprocal(
                            ssum[:, sb8 + 5:sb8 + 6], ssum[:, sb8 + scol:sb8 + scol + 1]),
                            reads=["ssum"], writes=["ssum"])

                    def _halves(ql):
                        qi = hf * QTH + ql
                        L = (qi + 1) * 128
                        nkc = L // 128
                        ha = max(1, min(nkc, 8) if nkc > 8 else (nkc + 1) // 2)
                        hv = [(0, ha, "pt_a"), (ha, nkc, "pt_b")] if ha < nkc else [(0, nkc, "pt_a")]
                        return nkc, hv

                    def stage2a(ql):
                        nkc, halves = _halves(ql)
                        sl = ql % 2
                        pb = sl * PMAX
                        for hi_, (k0, k1, ptok_) in enumerate(halves):
                            fns = [lambda e, kc=kc, pb=pb: e.transpose(
                                psb[kc // 8][:, (kc % 8) * 128:(kc % 8 + 1) * 128],
                                p_sb[:, pb + kc * 128:pb + (kc + 1) * 128], idb[:, :])
                                for kc in range(k0, k1)]
                            wr = sorted({f"psb{kc // 8}" for kc in range(k0, k1)})
                            P.mm_group(fns, reads=[f"p{sl}", "idb"], writes=wr)
                            if k1 <= 8 or k0 >= 8:
                                bsel = 0 if k1 <= 8 else 1
                                o0 = k0 - 8 * bsel
                                o1 = k1 - 8 * bsel
                                P.op("vector", lambda e, bsel=bsel, o0=o0, o1=o1, k0=k0: e.tensor_copy(
                                    pt_sb[:, k0 * 128:(k0 + (o1 - o0)) * 128],
                                    psb[bsel][:, o0 * 128:o1 * 128]),
                                    reads=[f"psb{bsel}"], writes=[ptok_, "pt"] if hi_ == 0 else [ptok_])
                            else:
                                P.op("vector", lambda e, k0=k0: e.tensor_copy(
                                    pt_sb[:, k0 * 128:1024], psb[0][:, k0 * 128:1024]),
                                    reads=["psb0"], writes=[ptok_, "pt"] if hi_ == 0 else [ptok_])
                                P.op("vector", lambda e, k1=k1: e.tensor_copy(
                                    pt_sb[:, 1024:k1 * 128], psb[1][:, 0:(k1 - 8) * 128]),
                                    reads=["psb1"], writes=[ptok_])
                    def stage2b(ql):
                        nkc, halves = _halves(ql)
                        sl = ql % 2
                        sb8 = sl * 8
                        pav = 4 + sl
                        for hi, (k0, k1, ptok_) in enumerate(halves):
                            fns = [lambda e, kc=kc, pav=pav, k0=k0, hi=hi: e.matmul(
                                ps[pav][:, 0:128], pt_sb[:, kc * 128:(kc + 1) * 128],
                                v_sb[:, kc * 128:(kc + 1) * 128],
                                start=(kc == k0 and hi == 0), stop=(kc == nkc - 1))
                                for kc in range(k0, k1)]
                            rds = [ptok_, "v_sb"] + (["pt"] if hi == len(halves) - 1 else [])
                            P.mm_group(fns, reads=rds, writes=[f"ps{pav}"])
                        P.op("scalar", lambda e, pav=pav, sl=sl, sb8=sb8: e.activation(
                            av_sb[:, sl * 128:(sl + 1) * 128], ps[pav][:, 0:128],
                            AF.Copy, scale=ssum[:, sb8 + 5:sb8 + 6]),
                            reads=[f"ps{pav}", "ssum"], writes=[f"av{sl}"])
                    def stage2c(ql):
                        sl = ql % 2
                        P.mm_group([lambda e, sl=sl: e.transpose(
                            psb[0][:, 0:128], av_sb[:, sl * 128:(sl + 1) * 128], idb[:, :])],
                            reads=[f"av{sl}", "idb"], writes=["psb0"])
                        P.op("scalar", lambda e, h=h, ql=ql: e.copy(
                            aoT[:, h * TH + ql * 128:h * TH + (ql + 1) * 128], psb[0][:, 0:128]),
                            reads=["psb0"], writes=["aoT"])

                    prev = None
                    pend_c = None
                    for ql in range(QTH):
                        stage1mm(ql)
                        if pend_c is not None:
                            stage2c(pend_c)
                            pend_c = None
                        if prev is not None:
                            stage2a(prev)
                            stage2b(prev)
                            pend_c = prev
                        stage1post(ql)
                        prev = ql
                    if pend_c is not None:
                        stage2c(pend_c)
                    stage2a(prev)
                    stage2b(prev)
                    stage2c(prev)
                for m in range(KC):
                    wp = state["wpar"] % 2
                    state["wpar"] += 1
                    wb, wt = wom_b[wp], f"wom{wp}"
                    P.op("gpsimd", lambda e, wb=wb, m=m: e.dma_start(
                        wb[:, :], wo_in[m * 128:(m + 1) * 128, :]), writes=[wt], dma=wt)
                    op_ = m % 2
                    wot = f"wo_out{op_}"
                    for ci, (noff, w) in enumerate(chunks(TH)):
                        pw = 2 * op_ + (ci % 2)
                        fns = [lambda e, cp=cp, pw=pw, wb=wb, noff=noff, w=w: e.matmul(
                            ps[pw][:, 0:w], wb[:, cp * 128:(cp + 1) * 128],
                            aoT[:, cp * TH + noff:cp * TH + noff + w],
                            start=(cp == 0), stop=(cp == KC - 1)) for cp in range(KC)]
                        P.mm_group(fns, reads=[wt, "aoT"], writes=[f"ps{pw}"])
                        P.op("scalar", lambda e, pw=pw, noff=noff, w=w, op_=op_: e.copy(
                            wo_out[op_][:, noff:noff + w], ps[pw][:, 0:w]),
                            reads=[f"ps{pw}"], writes=[wot])
                    P.op("sync", lambda e, m=m, b=b, qoff0=qoff0, op_=op_: e.dma_start(
                        aoh[m * 128:(m + 1) * 128, b * T + qoff0:b * T + qoff0 + TH], wo_out[op_][:, :]),
                        reads=[wot], writes=[f"aohr{b * 2 + hf}"], dma=wot)

    state["head"] = 0
    emit_ffn(x_in, lambda ti: None, xres, lambda ti: f"xr{ti}", with_ao=False)
    emit_attn()
    emit_ffn(xres, lambda ti: f"xr{ti}", out_t, lambda ti: f"out{ti}", with_ao=True)
    P.finalize()
    ctx.close()
    return nc


def host_prep(cfg, L0_attn_w_qkv, L0_attn_w_o, L0_ffn_gate_w, L0_ffn_w_gate, L0_ffn_w_up,
              L0_ffn_w_down, L0_ffn_sh_gate, L0_ffn_sh_up, L0_ffn_sh_down, L0_ffn_se_gate_w):
    Bc, T, D, NH, NE, H, HS = (cfg[k] for k in ("Bc", "T", "D", "NH", "NE", "H", "HS"))
    KC = D // 128
    MH = H // 128
    HS2 = HS // 2
    MSH = HS2 // 128
    CW = min(512, D)
    NCC = D // CW
    f32 = np.float32
    bfl = ml_dtypes.bfloat16
    w = {}
    w["wqkvh"] = np.ascontiguousarray(
        np.asarray(L0_attn_w_qkv, f32).reshape(KC, 128, 3, NH, 128)
        .transpose(3, 2, 1, 0, 4).reshape(NH * 3 * 128, KC * 128))
    w["wo"] = np.ascontiguousarray(
        np.asarray(L0_attn_w_o, f32).reshape(KC, 128, KC, 128)
        .transpose(2, 1, 0, 3).reshape(KC * 128, KC * 128))
    w["gateT"] = np.ascontiguousarray(
        np.concatenate([np.asarray(L0_ffn_gate_w, f32), np.asarray(L0_ffn_se_gate_w, f32)], axis=0).T)
    wg5 = np.asarray(L0_ffn_w_gate, f32).reshape(NE, MH, 128, KC, 128).transpose(0, 1, 4, 3, 2)
    wu5 = np.asarray(L0_ffn_w_up, f32).reshape(NE, MH, 128, KC, 128).transpose(0, 1, 4, 3, 2)
    w["wgu"] = np.ascontiguousarray(np.concatenate(
        [wg5.reshape(NE, MH, 128, KC * 128), wu5.reshape(NE, MH, 128, KC * 128)],
        axis=3).reshape(NE * MH * 128, 2 * KC * 128))
    w["wdc"] = np.ascontiguousarray(
        np.asarray(L0_ffn_w_down, f32).reshape(NE, NCC, CW, MH, 128)
        .transpose(0, 1, 4, 3, 2).reshape(NE * NCC * 128, MH * CW))
    sg5 = np.asarray(L0_ffn_sh_gate, f32).reshape(2, MSH, 128, KC, 128).transpose(0, 1, 4, 3, 2)
    su5 = np.asarray(L0_ffn_sh_up, f32).reshape(2, MSH, 128, KC, 128).transpose(0, 1, 4, 3, 2)
    w["shgu"] = np.ascontiguousarray(np.concatenate(
        [sg5.reshape(2, MSH, 128, KC * 128), su5.reshape(2, MSH, 128, KC * 128)],
        axis=3).reshape(2 * MSH * 128, 2 * KC * 128))
    w["shdc"] = np.ascontiguousarray(
        np.asarray(L0_ffn_sh_down, f32).T.reshape(2, MSH, 128, NCC, CW)
        .transpose(0, 3, 2, 1, 4).reshape(2 * NCC * 128, MSH * CW))
    c = np.zeros((128, NCONST), f32)
    c[:, CN_NEGRECD] = -1.0 / D
    c[:, CN_EPS] = 1e-5
    c[:, CN_RECD] = 1.0 / D
    c[:, CN_ISQH] = 1.0 / math.sqrt(128.0)
    c[:, CN_NEGSHIFT] = -EXP_SHIFT
    c[:, CN_HALF] = 0.5
    c[:, CN_NEG1] = -1.0
    w["consts"] = c
    w["idf"] = np.eye(128, dtype=f32)
    w["idb"] = np.eye(128, dtype=f32).astype(bfl)
    m = np.zeros((128, 128), f32)
    m[np.triu_indices(128, k=1)] = -1e30
    w["maskc"] = m
    pm = np.zeros((128, 128), f32)
    pm[(np.arange(128) + 64) % 128, np.arange(128)] = 1.0
    w["permb"] = pm.astype(bfl)
    inv = (1.0 / (10000.0 ** (np.arange(0, 128, 2, dtype=f32) / f32(128.0)))).astype(f32)
    t = np.arange(T, dtype=f32)
    fr = np.outer(t, inv).astype(f32)
    ct = np.zeros((128, T), f32)
    st = np.zeros((128, T), f32)
    ct[0:64] = np.cos(fr).T
    ct[64:128] = np.cos(fr).T
    st[0:64] = -np.sin(fr).T
    st[64:128] = np.sin(fr).T
    w["ctbl"] = ct
    w["stbl"] = st
    return w


_NC_CACHE = {}


def kernel(L0_attn_x, L0_attn_w_qkv, L0_attn_w_o, L0_ffn_gate_w, L0_ffn_w_gate, L0_ffn_w_up,
           L0_ffn_w_down, L0_ffn_sh_gate, L0_ffn_sh_up, L0_ffn_sh_down, L0_ffn_se_gate_w):
    cfg = REAL
    Bc, T, D = cfg["Bc"], cfg["T"], cfg["D"]
    n_cores = 8
    x = np.asarray(L0_attn_x, np.float32)
    B = x.shape[0]
    assert B == n_cores * Bc
    w = host_prep(cfg, L0_attn_w_qkv, L0_attn_w_o, L0_ffn_gate_w, L0_ffn_w_gate, L0_ffn_w_up,
                  L0_ffn_w_down, L0_ffn_sh_gate, L0_ffn_sh_up, L0_ffn_sh_down, L0_ffn_se_gate_w)
    key = tuple(sorted(cfg.items()))
    if key not in _NC_CACHE:
        _NC_CACHE[key] = build_nc(cfg)
    nc = _NC_CACHE[key]
    in_maps = []
    for c in range(n_cores):
        m = dict(w)
        m["x"] = np.ascontiguousarray(x[c * Bc:(c + 1) * Bc].reshape(Bc * T, D))
        in_maps.append(m)
    res = run_bass_kernel_spmd(nc, in_maps, core_ids=list(range(n_cores)))
    outs = [np.asarray(res.results[c]["out"], np.float32).reshape(Bc, T, D) for c in range(n_cores)]
    return np.concatenate(outs, axis=0)


# version 12
# speedup vs baseline: 1.0248x; 1.0248x over previous
import math
import numpy as np
import ml_dtypes
import concourse.bass as bass
import concourse.mybir as mybir
from concourse.bass_utils import run_bass_kernel_spmd

F32 = mybir.dt.float32
BF16 = mybir.dt.bfloat16
AF = mybir.ActivationFunctionType
ALU = mybir.AluOpType
AX = mybir.AxisListType

REAL = dict(Bc=2, T=2048, D=2048, NH=16, NE=8, H=1024, HS=2048, TG=1024)
EXP_SHIFT = 16.0

CN_NEGRECD, CN_EPS, CN_RECD, CN_ISQH, CN_NEGSHIFT, CN_HALF, CN_NEG1 = range(7)
NCONST = 8


def chunks(total, step=512):
    return [(o, min(step, total - o)) for o in range(0, total, step)]


class Prog:
    CE = ("tensor", "vector", "scalar", "gpsimd", "sync")

    def __init__(self, nc, ctx):
        self.nc = nc
        self.ctx = ctx
        self.sems = {}
        self.thunks = {e: [] for e in self.CE}
        self.cnt = {}
        self.sat = {e: {} for e in self.CE}
        self.stream_engine = {}
        self.buf = {}
        for e in self.CE[:4]:
            self._sem("P_" + e)

    def _sem(self, name):
        if name not in self.sems:
            self.sems[name] = self.ctx.enter_context(self.nc.semaphore(name))
            self.cnt[name] = 0
        return self.sems[name]

    def _wait(self, e, tag):
        if tag is None:
            return
        sem, val = tag
        if sem == "P_tensor" and e == "tensor":
            return
        if self.sat[e].get(sem, 0) >= val:
            return
        self.sat[e][sem] = val
        h = self.sems[sem]
        self.thunks[e].append(lambda eng, h=h, val=val: eng.wait_ge(h, val))

    def _deps(self, e, reads, writes):
        for t in reads:
            st = self.buf.get(t)
            if st:
                self._wait(e, st["w"])
        for t in writes:
            st = self.buf.get(t)
            if st:
                self._wait(e, st["w"])
                for r in st["r"]:
                    self._wait(e, r)

    def _record(self, tag, reads, writes):
        for t in writes:
            self.buf[t] = {"w": tag, "r": []}
        for t in reads:
            self.buf.setdefault(t, {"w": None, "r": []})["r"].append(tag)

    def op(self, e, fn, reads=(), writes=(), dma=None):
        self._deps(e, reads, writes)
        if dma is None:
            sem, step = "P_" + e, 1
        else:
            sem, step = "D_" + dma, 16
            self._sem(sem)
            self.stream_engine[sem] = e
        self.cnt[sem] += step
        val = self.cnt[sem]
        h = self.sems[sem]
        self.thunks[e].append(lambda eng, fn=fn, h=h, step=step: fn(eng).then_inc(h, step))
        self._record((sem, val), reads, writes)

    def mm_group(self, fns, reads=(), writes=()):
        e = "tensor"
        self._deps(e, reads, writes)
        self.cnt["P_tensor"] += 1
        val = self.cnt["P_tensor"]
        h = self.sems["P_tensor"]
        for fn in fns[:-1]:
            self.thunks[e].append(lambda eng, fn=fn: fn(eng))
        self.thunks[e].append(lambda eng, fn=fns[-1], h=h: fn(eng).then_inc(h, 1))
        self._record(("P_tensor", val), reads, writes)

    def switch_region(self, old_tokens, new_tokens):
        tags = []
        for t in list(old_tokens) + list(new_tokens):
            st = self.buf.pop(t, None)
            if st:
                if st["w"]:
                    tags.append(st["w"])
                tags.extend(st["r"])
        for t in new_tokens:
            self.buf[t] = {"w": None, "r": list(tags)}

    def finalize(self):
        nc = self.nc
        for sname, ename in self.stream_engine.items():
            h, val = self.sems[sname], self.cnt[sname]
            if val > self.sat[ename].get(sname, 0):
                self.thunks[ename].append(lambda eng, h=h, val=val: eng.wait_ge(h, val))
        with nc.Block() as block:
            for ename in self.CE:
                ths = self.thunks[ename]

                def runner(eng, ths=ths):
                    for th in ths:
                        th(eng)
                getattr(block, ename)(runner)


def build_nc(cfg):
    Bc, T, D, NH, NE, H, HS, TG = (cfg[k] for k in ("Bc", "T", "D", "NH", "NE", "H", "HS", "TG"))
    assert D == NH * 128
    KC = D // 128
    MH = H // 128
    HS2 = HS // 2
    MSH = HS2 // 128
    TGT = TG // 128
    NG = (Bc * T) // TG
    NT = (Bc * T) // 128
    TH = T // 2
    QTH = TH // 128

    nc = bass.Bass("TRN2", target_bir_lowering=False)

    CW = min(512, D)
    NCC = D // CW
    x_in = nc.dram_tensor("x", [Bc * T, D], F32, kind="ExternalInput").ap()
    wqkvh = nc.dram_tensor("wqkvh", [NH * 3 * 128, KC * 128], F32, kind="ExternalInput").ap()
    wo_in = nc.dram_tensor("wo", [KC * 128, KC * 128], F32, kind="ExternalInput").ap()
    gateT = nc.dram_tensor("gateT", [D, 9], F32, kind="ExternalInput").ap()
    wgu = nc.dram_tensor("wgu", [NE * MH * 128, 2 * KC * 128], F32, kind="ExternalInput").ap()
    wdc = nc.dram_tensor("wdc", [NE * NCC * 128, MH * CW], F32, kind="ExternalInput").ap()
    shgu = nc.dram_tensor("shgu", [2 * MSH * 128, 2 * KC * 128], F32, kind="ExternalInput").ap()
    shdc = nc.dram_tensor("shdc", [2 * NCC * 128, MSH * CW], F32, kind="ExternalInput").ap()
    consts_in = nc.dram_tensor("consts", [128, NCONST], F32, kind="ExternalInput").ap()
    idf_in = nc.dram_tensor("idf", [128, 128], F32, kind="ExternalInput").ap()
    idb_in = nc.dram_tensor("idb", [128, 128], BF16, kind="ExternalInput").ap()
    mask_in = nc.dram_tensor("maskc", [128, 128], F32, kind="ExternalInput").ap()
    perm_in = nc.dram_tensor("permb", [128, 128], BF16, kind="ExternalInput").ap()
    ctbl_in = nc.dram_tensor("ctbl", [128, T], F32, kind="ExternalInput").ap()
    stbl_in = nc.dram_tensor("stbl", [128, T], F32, kind="ExternalInput").ap()
    out_t = nc.dram_tensor("out", [Bc * T, D], F32, kind="ExternalOutput").ap()
    xres = nc.dram_tensor("xres", [Bc * T, D], F32, kind="Internal").ap()
    aoh = nc.dram_tensor("aoh", [D, Bc * T], BF16, kind="Internal").ap()

    def alloc(name, shape, dtype):
        return nc.alloc_sbuf_tensor(name, shape, dtype)

    def at(name, shape, dtype, base, byte_off):
        off = nc.lookup_mloc(base).addr + byte_off
        return nc.alloc_sbuf_tensor_at(name, shape, dtype, offset=off).ap()

    UA_B = max(TGT * D * 4, KC * T * 2)
    UB_B = max(KC * TG * 2, KC * TH * 2)
    wdb_B = MH * min(512, D) * 2
    UC_ffn = MH * TG * 2 + 2 * wdb_B
    UC_att = (TH + T + T + TH + T + TH + TH) * 2
    UC_B = max(UC_ffn, UC_att)
    wbuf_B = KC * 128 * 2
    UD_B = 4 * wbuf_B

    UA = alloc("UA", [128, UA_B], mybir.dt.uint8)
    UB = alloc("UB", [128, UB_B], mybir.dt.uint8)
    UC = alloc("UC", [128, UC_B], mybir.dt.uint8)
    UD = alloc("UD", [128, UD_B], mybir.dt.uint8)

    acc = at("acc", [128, TGT * D], F32, UA, 0)
    xlnT_a = at("xlnT_a", [128, KC * T], BF16, UA, 0)
    xlnT_f = at("xlnT_f", [128, KC * TG], BF16, UB, 0)
    aoT = at("aoT", [128, KC * TH], BF16, UB, 0)
    h_buf = at("h_buf", [128, MH * TG], BF16, UC, 0)
    wd_b = [at(f"wd{i}", [128, MH * min(512, D)], BF16, UC, MH * TG * 2 + i * wdb_B)
            for i in range(2)]
    _o = 0
    qrotT = at("qrotT", [128, TH], BF16, UC, _o); _o += TH * 2
    krotT = at("krotT", [128, T], BF16, UC, _o); _o += T * 2
    v_sb = at("v_sb", [128, T], BF16, UC, _o); _o += T * 2
    q_sb = at("q_sb", [128, TH], BF16, UC, _o); _o += TH * 2
    k_sb = at("k_sb", [128, T], BF16, UC, _o); _o += T * 2
    wo_out = [at(f"wo_out{i}", [128, TH], BF16, UC, _o + i * TH * 2) for i in range(2)]
    wgu_b = [at(f"wgu{i}", [128, 2 * KC * 128], BF16, UD, i * 2 * wbuf_B) for i in range(2)]
    qkvw_b = [at(f"qkvw{i}", [128, KC * 128], BF16, UD, i * wbuf_B) for i in range(2)]
    wom_b = [at(f"wom{i}", [128, KC * 128], BF16, UD, (2 + i) * wbuf_B) for i in range(2)]

    x_sb = [alloc(f"x{i}", [128, D], F32).ap() for i in range(2)]
    sg_sb = [alloc(f"sg{i}", [128, 512], F32).ap() for i in range(2)]
    PMAX = max(D, T)
    p_sb = alloc("p", [128, 2 * PMAX], BF16).ap()
    pt_sb = alloc("pt", [128, max(D, T)], BF16).ap()
    ao_sb = alloc("ao_sb", [128, D], BF16).ap()
    rsc = alloc("rsc", [128, 1024], F32).ap()
    ctbl = alloc("ctbl_sb", [128, T], F32).ap()
    stbl = alloc("stbl_sb", [128, T], F32).ap()
    consts = alloc("consts_sb", [128, NCONST], F32).ap()
    idf = alloc("idf_sb", [128, 128], F32).ap()
    idb = alloc("idb_sb", [128, 128], BF16).ap()
    maskb = alloc("mask_sb", [128, 128], F32).ap()
    permb = alloc("perm_sb", [128, 128], BF16).ap()
    gate_sb = alloc("gate_sb", [128, KC * 9], BF16).ap()
    rout_s = alloc("rout_s", [128, TGT * 9], F32).ap()
    rout = alloc("rout", [128, TGT * 9], F32).ap()
    rt = alloc("rt", [128, 32], F32).ap()
    rtv = alloc("rtv", [128, 8], F32).ap()
    sml = alloc("sml", [128, 16], F32).ap()
    ssum = alloc("ssum", [128, 16], F32).ap()
    av_sb = alloc("av_sb", [128, 256], BF16).ap()
    scT = alloc("scT", [16, TG], F32).ap()

    ps = [nc.alloc_psum_tensor(f"ps{i}", [128, 512], F32).ap() for i in range(6)]
    psb = [nc.alloc_psum_tensor(f"psb{i}", [128, 1024], BF16).ap() for i in range(2)]

    from contextlib import ExitStack
    ctx = ExitStack()
    P = Prog(nc, ctx)

    c_ap = lambda i: consts[:, i:i + 1]

    for dst, src, tok in ((consts, consts_in, "consts"), (idf, idf_in, "idf"),
                          (idb, idb_in, "idb"), (maskb, mask_in, "maskb"),
                          (permb, perm_in, "permb"), (ctbl, ctbl_in, "ctbl"),
                          (stbl, stbl_in, "stbl")):
        P.op("sync", lambda e, d=dst, s=src: e.dma_start(d[:, :], s[:, :]), writes=[tok], dma=tok)
    P.op("gpsimd", lambda e: e.dma_start(gate_sb[:, :], gateT.rearrange("(kc p) e -> p kc e", p=128)),
         writes=["gate_sb"], dma="gate_sb")

    state = {"gq": 0, "wpar": 0, "dpar": 0}

    def emit_ln_stats(ti, sl, src_ap, src_tok, with_ao):
        xb = sl
        xt = f"x{xb}"
        sb = 8 * sl
        smt = f"sml{sl}"
        pb = sl * PMAX
        ptok = f"p{sl}"
        P.op("sync", lambda e: e.dma_start(x_sb[xb][:, :], src_ap[ti * 128:(ti + 1) * 128, :]),
             reads=[src_tok] if src_tok else (), writes=[xt], dma=xt)
        if with_ao:
            gtok = ti * 128
            reg = (gtok // T) * 2 + ((gtok % T) // TH)
            P.op("sync", lambda e: e.dma_start_transpose(ao_sb[:, :], aoh[:, ti * 128:(ti + 1) * 128]),
                 reads=[f"aohr{reg}"], writes=["ao_sb"], dma="ao_sb")
            P.op("vector", lambda e: e.tensor_add(x_sb[xb][:, :], x_sb[xb][:, :], ao_sb[:, :]),
                 reads=[xt, "ao_sb"], writes=[xt])
        P.op("scalar", lambda e, pb=pb: e.activation(p_sb[:, pb:pb + D], x_sb[xb][:, :], AF.Copy,
                                                     accum_out=sml[:, sb + 1:sb + 2]),
             reads=[xt], writes=[ptok, smt])
        P.op("scalar", lambda e, pb=pb: e.activation(p_sb[:, pb:pb + D], x_sb[xb][:, :], AF.Square,
                                                     accum_out=sml[:, sb + 0:sb + 1]),
             reads=[xt], writes=[ptok, smt])
        P.op("vector", lambda e: e.tensor_scalar(sml[:, sb + 3:sb + 4], sml[:, sb + 1:sb + 2],
                                                 sml[:, sb + 1:sb + 2], c_ap(CN_NEGRECD),
                                                 ALU.mult, ALU.mult),
             reads=[smt, "consts"], writes=[smt])
        P.op("vector", lambda e: e.tensor_scalar(sml[:, sb + 2:sb + 3], sml[:, sb + 1:sb + 2],
                                                 c_ap(CN_NEGRECD), None, ALU.mult),
             reads=[smt, "consts"], writes=[smt])
        P.op("vector", lambda e: e.tensor_add(sml[:, sb + 4:sb + 5], sml[:, sb + 3:sb + 4],
                                              sml[:, sb + 0:sb + 1]),
             reads=[smt], writes=[smt])
        P.op("scalar", lambda e: e.activation(sml[:, sb + 5:sb + 6], sml[:, sb + 4:sb + 5],
                                              AF.Sqrt, scale=c_ap(CN_RECD), bias=c_ap(CN_EPS)),
             reads=[smt, "consts"], writes=[smt])
        P.op("vector", lambda e: e.reciprocal(sml[:, sb + 6:sb + 7], sml[:, sb + 5:sb + 6]),
             reads=[smt], writes=[smt])
        P.op("vector", lambda e, pb=pb: e.tensor_scalar(p_sb[:, pb:pb + D], x_sb[xb][:, :],
                                                        sml[:, sb + 2:sb + 3], sml[:, sb + 6:sb + 7],
                                                        ALU.add, ALU.mult),
             reads=[xt, smt], writes=[ptok])

    def emit_ln_transpose(sl, xlnT_ap, xlnT_tok, tcol):
        pb = sl * PMAX
        ptok = f"p{sl}"
        fns = [lambda e, kc=kc, pb=pb: e.transpose(
            psb[kc // 8][:, (kc % 8) * 128:(kc % 8 + 1) * 128],
            p_sb[:, pb + kc * 128:pb + (kc + 1) * 128], idb[:, :]) for kc in range(KC)]
        P.mm_group(fns, reads=[ptok, "idb"], writes=["psb0", "psb1"] if KC > 8 else ["psb0"])
        n0 = min(KC, 8)
        d3 = xlnT_ap.rearrange("q (kc t) -> q kc t", kc=KC)[:, 0:n0, tcol:tcol + 128]
        s3 = psb[0].rearrange("q (kc t) -> q kc t", kc=8)[:, 0:n0, :]
        P.op("scalar", lambda e, d3=d3, s3=s3: e.copy(d3, s3), reads=["psb0"], writes=[xlnT_tok])
        if KC > 8:
            d3b = xlnT_ap.rearrange("q (kc t) -> q kc t", kc=KC)[:, 8:KC, tcol:tcol + 128]
            s3b = psb[1].rearrange("q (kc t) -> q kc t", kc=8)[:, 0:KC - 8, :]
            P.op("scalar", lambda e, d3b=d3b, s3b=s3b: e.copy(d3b, s3b), reads=["psb1"], writes=[xlnT_tok])

    def ln_group_steps(tis, src_ap, src_tok_fn, xlnT_ap, xlnT_tok, tcols, with_ao):
        prev = None
        for i, ti in enumerate(tis):
            sl = i % 2
            emit_ln_stats(ti, sl, src_ap, src_tok_fn(ti), with_ao)
            if prev is not None:
                emit_ln_transpose(prev[1], xlnT_ap, xlnT_tok, prev[0])
            prev = (tcols[i], sl)
            yield
        emit_ln_transpose(prev[1], xlnT_ap, xlnT_tok, prev[0])
        yield

    def emit_ln_group(tis, src_ap, src_tok_fn, xlnT_ap, xlnT_tok, tcols, with_ao):
        for _ in ln_group_steps(tis, src_ap, src_tok_fn, xlnT_ap, xlnT_tok, tcols, with_ao):
            pass

    def emit_ffn(src_ap, src_tok_fn, dst_ap, dst_tok_fn, with_ao):
        P.switch_region(["xlnT_a"], [f"acc{t}" for t in range(TGT)])
        P.switch_region(["aoT"], ["xlnT_f"])
        P.switch_region(["qrotT", "krotT", "v_sb", "q_sb", "k_sb", "wo_out0", "wo_out1"],
                        [f"h{m}" for m in range(MH)] + ["wd0", "wd1"])
        P.switch_region(["qkvw0", "qkvw1", "wom0", "wom1"], ["wgu0", "wgu1"])
        emit_ln_group([t for t in range(TGT)], src_ap, src_tok_fn, xlnT_f, "xlnT_f",
                      [t * 128 for t in range(TGT)], with_ao)
        for g in range(NG):
            t0 = g * TGT
            for noff, w in chunks(TG):
                fns = [lambda e, kc=kc, noff=noff, w=w: e.matmul(
                    ps[0][0:9, 0:w], gate_sb[:, kc * 9:(kc + 1) * 9],
                    xlnT_f[:, kc * TG + noff:kc * TG + noff + w],
                    start=(kc == 0), stop=(kc == KC - 1)) for kc in range(KC)]
                P.mm_group(fns, reads=["gate_sb", "xlnT_f"], writes=["ps0"])
                P.op("scalar", lambda e, noff=noff, w=w: e.copy(
                    scT[0:9, noff:noff + w], ps[0][0:9, 0:w]), reads=["ps0"], writes=["scT"])
            for t in range(TGT):
                P.mm_group([lambda e, t=t: e.transpose(ps[1][:, 0:9],
                                                       scT[0:9, t * 128:(t + 1) * 128],
                                                       idf[0:9, 0:9])],
                           reads=["scT", "idf"], writes=["ps1"])
                P.op("scalar", lambda e, t=t: e.copy(rout_s[:, t * 9:(t + 1) * 9], ps[1][:, 0:9]),
                     reads=["ps1"], writes=["rout_s"])
            for t in range(TGT):
                s8 = rout_s[:, t * 9:t * 9 + 8]
                P.op("vector", lambda e, s8=s8: e.reduce_max(rtv[:, 0:1], s8, axis=AX.X),
                     reads=["rout_s"], writes=["rtv"])
                P.op("vector", lambda e, s8=s8: e.tensor_scalar(rt[:, 0:8], s8, rtv[:, 0:1], None, ALU.is_ge),
                     reads=["rout_s", "rtv"], writes=["rt"])
                P.op("vector", lambda e, s8=s8: e.scalar_tensor_tensor(rt[:, 8:16], rt[:, 0:8],
                                                                       -1e30, s8, ALU.mult, ALU.add),
                     reads=["rt", "rout_s"], writes=["rt"])
                P.op("vector", lambda e: e.reduce_max(rtv[:, 1:2], rt[:, 8:16], axis=AX.X),
                     reads=["rt"], writes=["rtv"])
                P.op("vector", lambda e: e.tensor_sub(rtv[:, 2:3], rtv[:, 0:1], rtv[:, 1:2]),
                     reads=["rtv"], writes=["rtv"])
                P.op("scalar", lambda e: e.activation(rtv[:, 3:4], rtv[:, 2:3], AF.Sigmoid),
                     reads=["rtv"], writes=["rtv"])
                P.op("scalar", lambda e: e.activation(rtv[:, 4:5], rtv[:, 2:3], AF.Sigmoid,
                                                      scale=c_ap(CN_NEG1)),
                     reads=["rtv", "consts"], writes=["rtv"])
                P.op("vector", lambda e: e.tensor_scalar(rt[:, 16:24], rt[:, 8:16], rtv[:, 1:2],
                                                         None, ALU.is_ge),
                     reads=["rt", "rtv"], writes=["rt"])
                P.op("vector", lambda e: e.tensor_scalar(rt[:, 24:32], rt[:, 16:24], rtv[:, 4:5],
                                                         None, ALU.mult),
                     reads=["rt", "rtv"], writes=["rt"])
                P.op("vector", lambda e, t=t: e.scalar_tensor_tensor(rout[:, t * 9:t * 9 + 8],
                                                                     rt[:, 0:8], rtv[:, 3:4],
                                                                     rt[:, 24:32], ALU.mult, ALU.add),
                     reads=["rt", "rtv"], writes=["rout"])
                P.op("scalar", lambda e, t=t: e.activation(rout[:, t * 9 + 8:t * 9 + 9],
                                                           rout_s[:, t * 9 + 8:t * 9 + 9], AF.Sigmoid),
                     reads=["rout_s"], writes=["rout"])

            def a_phase(wsrc, mh, chunk0):
                for m in range(mh):
                    wp = state["wpar"] % 2
                    state["wpar"] += 1
                    gb = wgu_b[wp]
                    gt = f"wgu{wp}"
                    r0 = (chunk0 + m) * 128
                    P.op("gpsimd", lambda e, gb=gb, r0=r0, wsrc=wsrc: e.dma_start(
                        gb[:, :], wsrc[r0:r0 + 128, :]), writes=[gt], dma=gt)
                    for ci, (noff, w) in enumerate(chunks(TG)):
                        par = (m * len(chunks(TG)) + ci) % 3
                        pg, pu = 2 * par, 2 * par + 1
                        fns = [lambda e, kc=kc, pg=pg, gb=gb, noff=noff, w=w: e.matmul(
                            ps[pg][:, 0:w], gb[:, kc * 128:(kc + 1) * 128],
                            xlnT_f[:, kc * TG + noff:kc * TG + noff + w],
                            start=(kc == 0), stop=(kc == KC - 1)) for kc in range(KC)]
                        P.mm_group(fns, reads=[gt, "xlnT_f"], writes=[f"ps{pg}"])
                        fns = [lambda e, kc=kc, pu=pu, gb=gb, noff=noff, w=w: e.matmul(
                            ps[pu][:, 0:w], gb[:, (KC + kc) * 128:(KC + kc + 1) * 128],
                            xlnT_f[:, kc * TG + noff:kc * TG + noff + w],
                            start=(kc == 0), stop=(kc == KC - 1)) for kc in range(KC)]
                        P.mm_group(fns, reads=[gt, "xlnT_f"], writes=[f"ps{pu}"])
                        sgp = par % 2
                        P.op("scalar", lambda e, sgp=sgp, pg=pg, w=w: e.activation(
                            sg_sb[sgp][:, 0:w], ps[pg][:, 0:w], AF.Silu),
                            reads=[f"ps{pg}"], writes=[f"sg{sgp}"])
                        P.op("vector", lambda e, sgp=sgp, pu=pu, m=m, noff=noff, w=w: e.tensor_mul(
                            h_buf[:, m * TG + noff:m * TG + noff + w], sg_sb[sgp][:, 0:w], ps[pu][:, 0:w]),
                            reads=[f"sg{sgp}", f"ps{pu}"], writes=[f"h{m}"])

            def b_phase(wsrc_d, mh, chunk0, scol_fn, first_write, filler=None):
                for cq_i, (cqoff, w) in enumerate(chunks(D)):
                    if filler is not None:
                        filler()
                    dp = state["dpar"] % 2
                    state["dpar"] += 1
                    db, dt_ = wd_b[dp], f"wd{dp}"
                    r0 = (chunk0 + cq_i) * 128
                    P.op("gpsimd", lambda e, db=db, r0=r0, mh=mh, w=w, wsrc_d=wsrc_d: e.dma_start(
                        db[:, 0:mh * w], wsrc_d[r0:r0 + 128, 0:mh * w]), writes=[dt_], dma=dt_)
                    for t in range(TGT):
                        pp = 2 + ((cq_i * TGT + t) % 4)
                        fns = [lambda e, m=m, pp=pp, db=db, t=t, w=w: e.matmul(
                            ps[pp][:, 0:w], h_buf[:, m * TG + t * 128:m * TG + (t + 1) * 128],
                            db[:, m * w:(m + 1) * w],
                            start=(m == 0), stop=(m == mh - 1)) for m in range(mh)]
                        P.mm_group(fns, reads=[f"h{m}" for m in range(mh)] + [dt_], writes=[f"ps{pp}"])
                        scol = scol_fn(t)
                        a_sl = acc[:, t * D + cqoff:t * D + cqoff + w]
                        if first_write:
                            P.op("vector", lambda e, pp=pp, scol=scol, a_sl=a_sl, w=w: e.tensor_scalar(
                                a_sl, ps[pp][:, 0:w], scol, None, ALU.mult),
                                reads=[f"ps{pp}", "rout"], writes=[f"acc{t}"])
                        else:
                            P.op("vector", lambda e, pp=pp, scol=scol, a_sl=a_sl, w=w: e.scalar_tensor_tensor(
                                a_sl, ps[pp][:, 0:w], scol, a_sl, ALU.mult, ALU.add),
                                reads=[f"ps{pp}", "rout", f"acc{t}"], writes=[f"acc{t}"])

            for ex in range(NE):
                a_phase(wgu, MH, ex * MH)
                b_phase(wdc, MH, ex * NCC, lambda t, ex=ex: rout[:, t * 9 + ex:t * 9 + ex + 1],
                        first_write=(ex == 0))
            ln_gen = None
            if g + 1 < NG:
                nt0 = (g + 1) * TGT
                ln_gen = ln_group_steps([nt0 + t for t in range(TGT)], src_ap, src_tok_fn,
                                        xlnT_f, "xlnT_f", [t * 128 for t in range(TGT)], with_ao)

            def filler(k=(2 * TGT + 2) // max(1, 2 * NCC) + 1):
                if ln_gen is not None:
                    for _ in range(k):
                        if next(ln_gen, "end") == "end":
                            break
            for hf in range(2):
                a_phase(shgu, MSH, hf * MSH)
                b_phase(shdc, MSH, hf * NCC, lambda t: rout[:, t * 9 + 8:t * 9 + 9],
                        first_write=False, filler=filler if hf == 1 else None)
            if ln_gen is not None:
                for _ in ln_gen:
                    pass

            for t in range(TGT):
                ti = t0 + t
                xb = t % 2
                xt = f"x{xb}"
                P.op("sync", lambda e, ti=ti, xb=xb: e.dma_start(
                    x_sb[xb][:, :], src_ap[ti * 128:(ti + 1) * 128, :]),
                    reads=[src_tok_fn(ti)] if src_tok_fn(ti) else (), writes=[xt], dma=xt)
                if with_ao:
                    reg = ((ti * 128) // T) * 2 + (((ti * 128) % T) // TH)
                    P.op("sync", lambda e, ti=ti: e.dma_start_transpose(
                        ao_sb[:, :], aoh[:, ti * 128:(ti + 1) * 128]),
                        reads=[f"aohr{reg}"], writes=["ao_sb"], dma="ao_sb")
                P.op("vector", lambda e, t=t, xb=xb: e.scalar_tensor_tensor(
                    x_sb[xb][:, :], acc[:, t * D:(t + 1) * D], c_ap(CN_HALF), x_sb[xb][:, :],
                    ALU.mult, ALU.add),
                    reads=[f"acc{t}", "consts", xt], writes=[xt])
                if with_ao:
                    P.op("vector", lambda e, xb=xb: e.tensor_add(x_sb[xb][:, :], x_sb[xb][:, :], ao_sb[:, :]),
                         reads=[xt, "ao_sb"], writes=[xt])
                P.op("sync", lambda e, ti=ti, xb=xb: e.dma_start(
                    dst_ap[ti * 128:(ti + 1) * 128, :], x_sb[xb][:, :]),
                    reads=[xt], writes=[dst_tok_fn(ti)], dma=xt)

    def emit_attn():
        P.switch_region([f"acc{t}" for t in range(TGT)], ["xlnT_a"])
        P.switch_region(["xlnT_f"], ["aoT"])
        P.switch_region([f"h{m}" for m in range(MH)] + ["wd0", "wd1"],
                        ["qrotT", "krotT", "v_sb", "q_sb", "k_sb", "wo_out0", "wo_out1"])
        P.switch_region(["wgu0", "wgu1"], ["qkvw0", "qkvw1", "wom0", "wom1"])
        pcyc = [0]

        def nps():
            i = pcyc[0] % 4
            pcyc[0] += 1
            return i

        emit_ln_group([tl for tl in range(T // 128)], xres, lambda ti: f"xr{ti}",
                      xlnT_a, "xlnT_a", [tl * 128 for tl in range(T // 128)], False)
        for b in range(Bc):
            bt0 = b * (T // 128)
            for hf in range(2):
                qoff0 = hf * TH
                kvlen = (hf + 1) * TH
                for h in range(NH):
                    plan = [(0, TH, qoff0, q_sb, "q_sb"), (1, kvlen, 0, k_sb, "k_sb")]
                    pchunks = {}
                    for which, length, poff, stage, stok in plan:
                        wp = state["wpar"] % 2
                        state["wpar"] += 1
                        wb, wt = qkvw_b[wp], f"qkvw{wp}"
                        rows = (h * 3 + which) * 128
                        P.op("gpsimd", lambda e, wb=wb, rows=rows: e.dma_start(
                            wb[:, :], wqkvh[rows:rows + 128, :]), writes=[wt], dma=wt)
                        lst = []
                        for noff, w in chunks(length):
                            pi = nps()
                            fns = [lambda e, kc=kc, pi=pi, wb=wb, noff=noff, w=w, poff=poff: e.matmul(
                                ps[pi][:, 0:w], wb[:, kc * 128:(kc + 1) * 128],
                                xlnT_a[:, kc * T + poff + noff:kc * T + poff + noff + w],
                                start=(kc == 0), stop=(kc == KC - 1)) for kc in range(KC)]
                            P.mm_group(fns, reads=[wt, "xlnT_a"], writes=[f"ps{pi}"])
                            P.op("scalar", lambda e, pi=pi, noff=noff, w=w, sb=stage: e.copy(
                                sb[:, noff:noff + w], ps[pi][:, 0:w]), reads=[f"ps{pi}"], writes=[stok])
                            lst.append((noff, w, pi))
                        pchunks[which] = lst
                    for which, dst, dst_tok, stage, stok, poff in (
                            (0, qrotT, "qrotT", q_sb, "q_sb", qoff0), (1, krotT, "krotT", k_sb, "k_sb", 0)):
                        for noff, w, _ in pchunks[which]:
                            pi = nps()
                            P.mm_group([lambda e, pi=pi, noff=noff, w=w, sb=stage: e.matmul(
                                ps[pi][:, 0:w], permb[:, :], sb[:, noff:noff + w], start=True, stop=True)],
                                reads=["permb", stok], writes=[f"ps{pi}"])
                            P.op("vector", lambda e, noff=noff, w=w, sb=stage, poff=poff: e.tensor_mul(
                                rsc[:, 0:w], sb[:, noff:noff + w], ctbl[:, poff + noff:poff + noff + w]),
                                reads=[stok, "ctbl"], writes=["rsc0"])
                            P.op("vector", lambda e, pi=pi, noff=noff, w=w, poff=poff: e.tensor_mul(
                                rsc[:, 512:512 + w], ps[pi][:, 0:w], stbl[:, poff + noff:poff + noff + w]),
                                reads=[f"ps{pi}", "stbl"], writes=["rsc1"])
                            P.op("vector", lambda e, noff=noff, w=w, dst=dst: e.tensor_add(
                                dst[:, noff:noff + w], rsc[:, 0:w], rsc[:, 512:512 + w]),
                                reads=["rsc0", "rsc1"], writes=[dst_tok])
                    wp = state["wpar"] % 2
                    state["wpar"] += 1
                    wb, wt = qkvw_b[wp], f"qkvw{wp}"
                    rows = (h * 3 + 2) * 128
                    P.op("gpsimd", lambda e, wb=wb, rows=rows: e.dma_start(
                        wb[:, :], wqkvh[rows:rows + 128, :]), writes=[wt], dma=wt)
                    for noff, w in chunks(kvlen):
                        pi = nps()
                        fns = [lambda e, kc=kc, pi=pi, wb=wb, noff=noff, w=w: e.matmul(
                            ps[pi][:, 0:w], wb[:, kc * 128:(kc + 1) * 128],
                            xlnT_a[:, kc * T + noff:kc * T + noff + w],
                            start=(kc == 0), stop=(kc == KC - 1)) for kc in range(KC)]
                        P.mm_group(fns, reads=[wt, "xlnT_a"], writes=[f"ps{pi}"])
                        P.op("scalar", lambda e, pi=pi, noff=noff, w=w: e.copy(
                            pt_sb[:, noff:noff + w], ps[pi][:, 0:w]), reads=[f"ps{pi}"], writes=["pt"])
                    nvc = kvlen // 128
                    fns = [lambda e, kc=kc: e.transpose(
                        psb[kc // 8][:, (kc % 8) * 128:(kc % 8 + 1) * 128],
                        pt_sb[:, kc * 128:(kc + 1) * 128], idb[:, :]) for kc in range(nvc)]
                    P.mm_group(fns, reads=["pt", "idb"],
                               writes=["psb0", "psb1"] if nvc > 8 else ["psb0"])
                    n0 = min(nvc, 8)
                    P.op("scalar", lambda e, n0=n0: e.copy(v_sb[:, 0:n0 * 128], psb[0][:, 0:n0 * 128]),
                         reads=["psb0"], writes=["v_sb"])
                    if nvc > 8:
                        n1 = nvc - 8
                        P.op("scalar", lambda e, n1=n1: e.copy(
                            v_sb[:, 1024:1024 + n1 * 128], psb[1][:, 0:n1 * 128]),
                            reads=["psb1"], writes=["v_sb"])

                    def s1bank(ql, ci):
                        return (ci + 2 * (ql % 2)) % 4

                    def stage1mm(ql):
                        qi = hf * QTH + ql
                        L = (qi + 1) * 128
                        scs = chunks(L)
                        for ci, (noff, w) in enumerate(scs):
                            pc = s1bank(ql, ci)
                            P.mm_group([lambda e, pc=pc, noff=noff, w=w, ql=ql: e.matmul(
                                ps[pc][:, 0:w], qrotT[:, ql * 128:(ql + 1) * 128],
                                krotT[:, noff:noff + w], start=True, stop=True)],
                                reads=["qrotT", "krotT"], writes=[f"ps{pc}"])

                    def stage1post(ql):
                        qi = hf * QTH + ql
                        L = (qi + 1) * 128
                        scs = chunks(L)
                        sl = ql % 2
                        pb = sl * PMAX
                        sb8 = sl * 8
                        dci, doff = s1bank(ql, (L - 128) // 512), (L - 128) % 512
                        P.op("vector", lambda e, dci=dci, doff=doff: e.tensor_add(
                            ps[dci][:, doff:doff + 128], ps[dci][:, doff:doff + 128], maskb[:, :]),
                            reads=[f"ps{dci}", "maskb"], writes=[f"ps{dci}"])
                        for ci, (noff, w) in enumerate(scs):
                            pc = s1bank(ql, ci)
                            P.op("scalar", lambda e, pc=pc, ci=ci, noff=noff, w=w, pb=pb, sb8=sb8: e.activation(
                                p_sb[:, pb + noff:pb + noff + w], ps[pc][:, 0:w], AF.Exp,
                                scale=c_ap(CN_ISQH), bias=c_ap(CN_NEGSHIFT),
                                accum_out=ssum[:, sb8 + ci:sb8 + ci + 1]),
                                reads=[f"ps{pc}", "consts"], writes=[f"p{sl}", "ssum"])
                        ncs = len(scs)
                        if ncs > 1:
                            P.op("vector", lambda e, ncs=ncs, sb8=sb8: e.reduce_sum(
                                ssum[:, sb8 + 4:sb8 + 5], ssum[:, sb8:sb8 + ncs], axis=AX.X),
                                reads=["ssum"], writes=["ssum"])
                            scol = 4
                        else:
                            scol = 0
                        P.op("vector", lambda e, scol=scol, sb8=sb8: e.reciprocal(
                            ssum[:, sb8 + 5:sb8 + 6], ssum[:, sb8 + scol:sb8 + scol + 1]),
                            reads=["ssum"], writes=["ssum"])

                    def _halves(ql):
                        qi = hf * QTH + ql
                        L = (qi + 1) * 128
                        nkc = L // 128
                        ha = max(1, min(nkc, 8) if nkc > 8 else (nkc + 1) // 2)
                        hv = [(0, ha, "pt_a"), (ha, nkc, "pt_b")] if ha < nkc else [(0, nkc, "pt_a")]
                        return nkc, hv

                    def stage2a(ql):
                        nkc, halves = _halves(ql)
                        sl = ql % 2
                        pb = sl * PMAX
                        for hi_, (k0, k1, ptok_) in enumerate(halves):
                            fns = [lambda e, kc=kc, pb=pb: e.transpose(
                                psb[kc // 8][:, (kc % 8) * 128:(kc % 8 + 1) * 128],
                                p_sb[:, pb + kc * 128:pb + (kc + 1) * 128], idb[:, :])
                                for kc in range(k0, k1)]
                            wr = sorted({f"psb{kc // 8}" for kc in range(k0, k1)})
                            P.mm_group(fns, reads=[f"p{sl}", "idb"], writes=wr)
                            if k1 <= 8 or k0 >= 8:
                                bsel = 0 if k1 <= 8 else 1
                                o0 = k0 - 8 * bsel
                                o1 = k1 - 8 * bsel
                                P.op("vector", lambda e, bsel=bsel, o0=o0, o1=o1, k0=k0: e.tensor_copy(
                                    pt_sb[:, k0 * 128:(k0 + (o1 - o0)) * 128],
                                    psb[bsel][:, o0 * 128:o1 * 128]),
                                    reads=[f"psb{bsel}"], writes=[ptok_, "pt"] if hi_ == 0 else [ptok_])
                            else:
                                P.op("vector", lambda e, k0=k0: e.tensor_copy(
                                    pt_sb[:, k0 * 128:1024], psb[0][:, k0 * 128:1024]),
                                    reads=["psb0"], writes=[ptok_, "pt"] if hi_ == 0 else [ptok_])
                                P.op("vector", lambda e, k1=k1: e.tensor_copy(
                                    pt_sb[:, 1024:k1 * 128], psb[1][:, 0:(k1 - 8) * 128]),
                                    reads=["psb1"], writes=[ptok_])
                    def stage2b(ql):
                        nkc, halves = _halves(ql)
                        sl = ql % 2
                        sb8 = sl * 8
                        pav = 4 + sl
                        for hi, (k0, k1, ptok_) in enumerate(halves):
                            fns = [lambda e, kc=kc, pav=pav, k0=k0, hi=hi: e.matmul(
                                ps[pav][:, 0:128], pt_sb[:, kc * 128:(kc + 1) * 128],
                                v_sb[:, kc * 128:(kc + 1) * 128],
                                start=(kc == k0 and hi == 0), stop=(kc == nkc - 1))
                                for kc in range(k0, k1)]
                            rds = [ptok_, "v_sb"] + (["pt"] if hi == len(halves) - 1 else [])
                            P.mm_group(fns, reads=rds, writes=[f"ps{pav}"])
                        P.op("scalar", lambda e, pav=pav, sl=sl, sb8=sb8: e.activation(
                            av_sb[:, sl * 128:(sl + 1) * 128], ps[pav][:, 0:128],
                            AF.Copy, scale=ssum[:, sb8 + 5:sb8 + 6]),
                            reads=[f"ps{pav}", "ssum"], writes=[f"av{sl}"])
                    def stage2c(ql):
                        sl = ql % 2
                        P.mm_group([lambda e, sl=sl: e.transpose(
                            psb[0][:, 0:128], av_sb[:, sl * 128:(sl + 1) * 128], idb[:, :])],
                            reads=[f"av{sl}", "idb"], writes=["psb0"])
                        P.op("scalar", lambda e, h=h, ql=ql: e.copy(
                            aoT[:, h * TH + ql * 128:h * TH + (ql + 1) * 128], psb[0][:, 0:128]),
                            reads=["psb0"], writes=["aoT"])

                    prev = None
                    pend_c = None
                    for ql in range(QTH):
                        stage1mm(ql)
                        if pend_c is not None:
                            stage2c(pend_c)
                            pend_c = None
                        if prev is not None:
                            stage2a(prev)
                            stage2b(prev)
                            pend_c = prev
                        stage1post(ql)
                        prev = ql
                    if pend_c is not None:
                        stage2c(pend_c)
                    stage2a(prev)
                    stage2b(prev)
                    stage2c(prev)
                ln_gen_a = None
                if hf == 1 and b + 1 < Bc:
                    nb0 = (b + 1) * (T // 128)
                    ln_gen_a = ln_group_steps(
                        [nb0 + tl for tl in range(T // 128)], xres, lambda ti: f"xr{ti}",
                        xlnT_a, "xlnT_a", [tl * 128 for tl in range(T // 128)], False)
                for m in range(KC):
                    if ln_gen_a is not None:
                        for _ in range(2):
                            if next(ln_gen_a, "end") == "end":
                                ln_gen_a = None
                                break
                    wp = state["wpar"] % 2
                    state["wpar"] += 1
                    wb, wt = wom_b[wp], f"wom{wp}"
                    P.op("gpsimd", lambda e, wb=wb, m=m: e.dma_start(
                        wb[:, :], wo_in[m * 128:(m + 1) * 128, :]), writes=[wt], dma=wt)
                    op_ = m % 2
                    wot = f"wo_out{op_}"
                    for ci, (noff, w) in enumerate(chunks(TH)):
                        pw = 2 * op_ + (ci % 2)
                        fns = [lambda e, cp=cp, pw=pw, wb=wb, noff=noff, w=w: e.matmul(
                            ps[pw][:, 0:w], wb[:, cp * 128:(cp + 1) * 128],
                            aoT[:, cp * TH + noff:cp * TH + noff + w],
                            start=(cp == 0), stop=(cp == KC - 1)) for cp in range(KC)]
                        P.mm_group(fns, reads=[wt, "aoT"], writes=[f"ps{pw}"])
                        P.op("scalar", lambda e, pw=pw, noff=noff, w=w, op_=op_: e.copy(
                            wo_out[op_][:, noff:noff + w], ps[pw][:, 0:w]),
                            reads=[f"ps{pw}"], writes=[wot])
                    P.op("sync", lambda e, m=m, b=b, qoff0=qoff0, op_=op_: e.dma_start(
                        aoh[m * 128:(m + 1) * 128, b * T + qoff0:b * T + qoff0 + TH], wo_out[op_][:, :]),
                        reads=[wot], writes=[f"aohr{b * 2 + hf}"], dma=wot)
                if ln_gen_a is not None:
                    for _ in ln_gen_a:
                        pass

    state["head"] = 0
    emit_ffn(x_in, lambda ti: None, xres, lambda ti: f"xr{ti}", with_ao=False)
    emit_attn()
    emit_ffn(xres, lambda ti: f"xr{ti}", out_t, lambda ti: f"out{ti}", with_ao=True)
    P.finalize()
    ctx.close()
    return nc


def host_prep(cfg, L0_attn_w_qkv, L0_attn_w_o, L0_ffn_gate_w, L0_ffn_w_gate, L0_ffn_w_up,
              L0_ffn_w_down, L0_ffn_sh_gate, L0_ffn_sh_up, L0_ffn_sh_down, L0_ffn_se_gate_w):
    Bc, T, D, NH, NE, H, HS = (cfg[k] for k in ("Bc", "T", "D", "NH", "NE", "H", "HS"))
    KC = D // 128
    MH = H // 128
    HS2 = HS // 2
    MSH = HS2 // 128
    CW = min(512, D)
    NCC = D // CW
    f32 = np.float32
    bfl = ml_dtypes.bfloat16
    w = {}
    w["wqkvh"] = np.ascontiguousarray(
        np.asarray(L0_attn_w_qkv, f32).reshape(KC, 128, 3, NH, 128)
        .transpose(3, 2, 1, 0, 4).reshape(NH * 3 * 128, KC * 128))
    w["wo"] = np.ascontiguousarray(
        np.asarray(L0_attn_w_o, f32).reshape(KC, 128, KC, 128)
        .transpose(2, 1, 0, 3).reshape(KC * 128, KC * 128))
    w["gateT"] = np.ascontiguousarray(
        np.concatenate([np.asarray(L0_ffn_gate_w, f32), np.asarray(L0_ffn_se_gate_w, f32)], axis=0).T)
    wg5 = np.asarray(L0_ffn_w_gate, f32).reshape(NE, MH, 128, KC, 128).transpose(0, 1, 4, 3, 2)
    wu5 = np.asarray(L0_ffn_w_up, f32).reshape(NE, MH, 128, KC, 128).transpose(0, 1, 4, 3, 2)
    w["wgu"] = np.ascontiguousarray(np.concatenate(
        [wg5.reshape(NE, MH, 128, KC * 128), wu5.reshape(NE, MH, 128, KC * 128)],
        axis=3).reshape(NE * MH * 128, 2 * KC * 128))
    w["wdc"] = np.ascontiguousarray(
        np.asarray(L0_ffn_w_down, f32).reshape(NE, NCC, CW, MH, 128)
        .transpose(0, 1, 4, 3, 2).reshape(NE * NCC * 128, MH * CW))
    sg5 = np.asarray(L0_ffn_sh_gate, f32).reshape(2, MSH, 128, KC, 128).transpose(0, 1, 4, 3, 2)
    su5 = np.asarray(L0_ffn_sh_up, f32).reshape(2, MSH, 128, KC, 128).transpose(0, 1, 4, 3, 2)
    w["shgu"] = np.ascontiguousarray(np.concatenate(
        [sg5.reshape(2, MSH, 128, KC * 128), su5.reshape(2, MSH, 128, KC * 128)],
        axis=3).reshape(2 * MSH * 128, 2 * KC * 128))
    w["shdc"] = np.ascontiguousarray(
        np.asarray(L0_ffn_sh_down, f32).T.reshape(2, MSH, 128, NCC, CW)
        .transpose(0, 3, 2, 1, 4).reshape(2 * NCC * 128, MSH * CW))
    c = np.zeros((128, NCONST), f32)
    c[:, CN_NEGRECD] = -1.0 / D
    c[:, CN_EPS] = 1e-5
    c[:, CN_RECD] = 1.0 / D
    c[:, CN_ISQH] = 1.0 / math.sqrt(128.0)
    c[:, CN_NEGSHIFT] = -EXP_SHIFT
    c[:, CN_HALF] = 0.5
    c[:, CN_NEG1] = -1.0
    w["consts"] = c
    w["idf"] = np.eye(128, dtype=f32)
    w["idb"] = np.eye(128, dtype=f32).astype(bfl)
    m = np.zeros((128, 128), f32)
    m[np.triu_indices(128, k=1)] = -1e30
    w["maskc"] = m
    pm = np.zeros((128, 128), f32)
    pm[(np.arange(128) + 64) % 128, np.arange(128)] = 1.0
    w["permb"] = pm.astype(bfl)
    inv = (1.0 / (10000.0 ** (np.arange(0, 128, 2, dtype=f32) / f32(128.0)))).astype(f32)
    t = np.arange(T, dtype=f32)
    fr = np.outer(t, inv).astype(f32)
    ct = np.zeros((128, T), f32)
    st = np.zeros((128, T), f32)
    ct[0:64] = np.cos(fr).T
    ct[64:128] = np.cos(fr).T
    st[0:64] = -np.sin(fr).T
    st[64:128] = np.sin(fr).T
    w["ctbl"] = ct
    w["stbl"] = st
    return w


_NC_CACHE = {}


def kernel(L0_attn_x, L0_attn_w_qkv, L0_attn_w_o, L0_ffn_gate_w, L0_ffn_w_gate, L0_ffn_w_up,
           L0_ffn_w_down, L0_ffn_sh_gate, L0_ffn_sh_up, L0_ffn_sh_down, L0_ffn_se_gate_w):
    cfg = REAL
    Bc, T, D = cfg["Bc"], cfg["T"], cfg["D"]
    n_cores = 8
    x = np.asarray(L0_attn_x, np.float32)
    B = x.shape[0]
    assert B == n_cores * Bc
    w = host_prep(cfg, L0_attn_w_qkv, L0_attn_w_o, L0_ffn_gate_w, L0_ffn_w_gate, L0_ffn_w_up,
                  L0_ffn_w_down, L0_ffn_sh_gate, L0_ffn_sh_up, L0_ffn_sh_down, L0_ffn_se_gate_w)
    key = tuple(sorted(cfg.items()))
    if key not in _NC_CACHE:
        _NC_CACHE[key] = build_nc(cfg)
    nc = _NC_CACHE[key]
    in_maps = []
    for c in range(n_cores):
        m = dict(w)
        m["x"] = np.ascontiguousarray(x[c * Bc:(c + 1) * Bc].reshape(Bc * T, D))
        in_maps.append(m)
    res = run_bass_kernel_spmd(nc, in_maps, core_ids=list(range(n_cores)))
    outs = [np.asarray(res.results[c]["out"], np.float32).reshape(Bc, T, D) for c in range(n_cores)]
    return np.concatenate(outs, axis=0)


# version 13
# speedup vs baseline: 1.0789x; 1.0528x over previous
import math
import numpy as np
import ml_dtypes
import concourse.bass as bass
import concourse.mybir as mybir
from concourse.bass_utils import run_bass_kernel_spmd

F32 = mybir.dt.float32
BF16 = mybir.dt.bfloat16
AF = mybir.ActivationFunctionType
ALU = mybir.AluOpType
AX = mybir.AxisListType

REAL = dict(Bc=2, T=2048, D=2048, NH=16, NE=8, H=1024, HS=2048, TG=1024)
EXP_SHIFT = 16.0

CN_NEGRECD, CN_EPS, CN_RECD, CN_ISQH, CN_NEGSHIFT, CN_HALF, CN_NEG1 = range(7)
NCONST = 8


def chunks(total, step=512):
    return [(o, min(step, total - o)) for o in range(0, total, step)]


class Prog:
    CE = ("tensor", "vector", "scalar", "gpsimd", "sync")

    def __init__(self, nc, ctx):
        self.nc = nc
        self.ctx = ctx
        self.sems = {}
        self.thunks = {e: [] for e in self.CE}
        self.cnt = {}
        self.sat = {e: {} for e in self.CE}
        self.stream_engine = {}
        self.buf = {}
        for e in self.CE[:4]:
            self._sem("P_" + e)

    def _sem(self, name):
        if name not in self.sems:
            self.sems[name] = self.ctx.enter_context(self.nc.semaphore(name))
            self.cnt[name] = 0
        return self.sems[name]

    def _wait(self, e, tag):
        if tag is None:
            return
        sem, val = tag
        if sem == "P_tensor" and e == "tensor":
            return
        if self.sat[e].get(sem, 0) >= val:
            return
        self.sat[e][sem] = val
        h = self.sems[sem]
        self.thunks[e].append(lambda eng, h=h, val=val: eng.wait_ge(h, val))

    def _deps(self, e, reads, writes):
        for t in reads:
            st = self.buf.get(t)
            if st:
                self._wait(e, st["w"])
        for t in writes:
            st = self.buf.get(t)
            if st:
                self._wait(e, st["w"])
                for r in st["r"]:
                    self._wait(e, r)

    def _record(self, tag, reads, writes):
        for t in writes:
            self.buf[t] = {"w": tag, "r": []}
        for t in reads:
            self.buf.setdefault(t, {"w": None, "r": []})["r"].append(tag)

    def op(self, e, fn, reads=(), writes=(), dma=None):
        self._deps(e, reads, writes)
        if dma is None:
            sem, step = "P_" + e, 1
        else:
            sem, step = "D_" + dma, 16
            self._sem(sem)
            self.stream_engine[sem] = e
        self.cnt[sem] += step
        val = self.cnt[sem]
        h = self.sems[sem]
        self.thunks[e].append(lambda eng, fn=fn, h=h, step=step: fn(eng).then_inc(h, step))
        self._record((sem, val), reads, writes)

    def mm_group(self, fns, reads=(), writes=()):
        e = "tensor"
        self._deps(e, reads, writes)
        self.cnt["P_tensor"] += 1
        val = self.cnt["P_tensor"]
        h = self.sems["P_tensor"]
        for fn in fns[:-1]:
            self.thunks[e].append(lambda eng, fn=fn: fn(eng))
        self.thunks[e].append(lambda eng, fn=fns[-1], h=h: fn(eng).then_inc(h, 1))
        self._record(("P_tensor", val), reads, writes)

    def switch_region(self, old_tokens, new_tokens):
        tags = []
        for t in list(old_tokens) + list(new_tokens):
            st = self.buf.pop(t, None)
            if st:
                if st["w"]:
                    tags.append(st["w"])
                tags.extend(st["r"])
        for t in new_tokens:
            self.buf[t] = {"w": None, "r": list(tags)}

    def finalize(self):
        nc = self.nc
        for sname, ename in self.stream_engine.items():
            h, val = self.sems[sname], self.cnt[sname]
            if val > self.sat[ename].get(sname, 0):
                self.thunks[ename].append(lambda eng, h=h, val=val: eng.wait_ge(h, val))
        with nc.Block() as block:
            for ename in self.CE:
                ths = self.thunks[ename]

                def runner(eng, ths=ths):
                    for th in ths:
                        th(eng)
                getattr(block, ename)(runner)


def build_nc(cfg):
    Bc, T, D, NH, NE, H, HS, TG = (cfg[k] for k in ("Bc", "T", "D", "NH", "NE", "H", "HS", "TG"))
    assert D == NH * 128
    KC = D // 128
    MH = H // 128
    HS2 = HS // 2
    MSH = HS2 // 128
    TGT = TG // 128
    NG = (Bc * T) // TG
    NT = (Bc * T) // 128
    TH = T // 2
    QTH = TH // 128

    nc = bass.Bass("TRN2", target_bir_lowering=False)

    CW = min(512, D)
    NCC = D // CW
    x_in = nc.dram_tensor("x", [Bc * T, D], F32, kind="ExternalInput").ap()
    wqkvh = nc.dram_tensor("wqkvh", [NH * 3 * 128, KC * 128], F32, kind="ExternalInput").ap()
    wo_in = nc.dram_tensor("wo", [KC * 128, KC * 128], F32, kind="ExternalInput").ap()
    gateT = nc.dram_tensor("gateT", [D, 9], F32, kind="ExternalInput").ap()
    wgu = nc.dram_tensor("wgu", [NE * MH * 128, 2 * KC * 128], F32, kind="ExternalInput").ap()
    wdc = nc.dram_tensor("wdc", [NE * NCC * 128, MH * CW], F32, kind="ExternalInput").ap()
    shgu = nc.dram_tensor("shgu", [2 * MSH * 128, 2 * KC * 128], F32, kind="ExternalInput").ap()
    shdc = nc.dram_tensor("shdc", [2 * NCC * 128, MSH * CW], F32, kind="ExternalInput").ap()
    consts_in = nc.dram_tensor("consts", [128, NCONST], F32, kind="ExternalInput").ap()
    idf_in = nc.dram_tensor("idf", [128, 128], F32, kind="ExternalInput").ap()
    idb_in = nc.dram_tensor("idb", [128, 128], BF16, kind="ExternalInput").ap()
    mask_in = nc.dram_tensor("maskc", [128, 128], F32, kind="ExternalInput").ap()
    perm_in = nc.dram_tensor("permb", [128, 128], BF16, kind="ExternalInput").ap()
    ctbl_in = nc.dram_tensor("ctbl", [128, T], F32, kind="ExternalInput").ap()
    stbl_in = nc.dram_tensor("stbl", [128, T], F32, kind="ExternalInput").ap()
    out_t = nc.dram_tensor("out", [Bc * T, D], F32, kind="ExternalOutput").ap()
    xres = nc.dram_tensor("xres", [Bc * T, D], F32, kind="Internal").ap()
    aoh = nc.dram_tensor("aoh", [D, Bc * T], BF16, kind="Internal").ap()

    def alloc(name, shape, dtype):
        return nc.alloc_sbuf_tensor(name, shape, dtype)

    def at(name, shape, dtype, base, byte_off):
        off = nc.lookup_mloc(base).addr + byte_off
        return nc.alloc_sbuf_tensor_at(name, shape, dtype, offset=off).ap()

    UA_B = max(TGT * D * 4, KC * T * 2)
    UB_B = max(KC * TG * 2, KC * TH * 2)
    wdb_B = MH * min(512, D) * 2
    UC_ffn = MH * TG * 2 + 2 * wdb_B
    UC_att = (TH + T + T + TH + T + TH + TH) * 2
    UC_B = max(UC_ffn, UC_att)
    wbuf_B = KC * 128 * 2
    UD_B = 4 * wbuf_B

    UA = alloc("UA", [128, UA_B], mybir.dt.uint8)
    UB = alloc("UB", [128, UB_B], mybir.dt.uint8)
    UC = alloc("UC", [128, UC_B], mybir.dt.uint8)
    UD = alloc("UD", [128, UD_B], mybir.dt.uint8)

    acc = at("acc", [128, TGT * D], F32, UA, 0)
    xlnT_a = at("xlnT_a", [128, KC * T], BF16, UA, 0)
    xlnT_f = at("xlnT_f", [128, KC * TG], BF16, UB, 0)
    aoT = at("aoT", [128, KC * TH], BF16, UB, 0)
    h_buf = at("h_buf", [128, MH * TG], BF16, UC, 0)
    wd_b = [at(f"wd{i}", [128, MH * min(512, D)], BF16, UC, MH * TG * 2 + i * wdb_B)
            for i in range(2)]
    _o = 0
    qrotT = at("qrotT", [128, TH], BF16, UC, _o); _o += TH * 2
    krotT = at("krotT", [128, T], BF16, UC, _o); _o += T * 2
    v_sb = at("v_sb", [128, T], BF16, UC, _o); _o += T * 2
    q_sb = at("q_sb", [128, TH], BF16, UC, _o); _o += TH * 2
    k_sb = at("k_sb", [128, T], BF16, UC, _o); _o += T * 2
    wo_out = [at(f"wo_out{i}", [128, TH], BF16, UC, _o + i * TH * 2) for i in range(2)]
    wgu_b = [at(f"wgu{i}", [128, 2 * KC * 128], BF16, UD, i * 2 * wbuf_B) for i in range(2)]
    qkvw_b = [at(f"qkvw{i}", [128, KC * 128], BF16, UD, i * wbuf_B) for i in range(2)]
    wom_b = [at(f"wom{i}", [128, KC * 128], BF16, UD, (2 + i) * wbuf_B) for i in range(2)]

    x_sb = [alloc(f"x{i}", [128, D], F32).ap() for i in range(2)]
    sg_sb = [alloc(f"sg{i}", [128, 512], F32).ap() for i in range(2)]
    PMAX = max(D, T)
    p_sb = alloc("p", [128, 2 * PMAX], BF16).ap()
    pt_sb = alloc("pt", [128, max(D, T)], BF16).ap()
    ao_sb = alloc("ao_sb", [128, D], BF16).ap()
    rsc = alloc("rsc", [128, 1024], F32).ap()
    ctbl = alloc("ctbl_sb", [128, T], F32).ap()
    stbl = alloc("stbl_sb", [128, T], F32).ap()
    consts = alloc("consts_sb", [128, NCONST], F32).ap()
    idf = alloc("idf_sb", [128, 128], F32).ap()
    idb = alloc("idb_sb", [128, 128], BF16).ap()
    maskb = alloc("mask_sb", [128, 128], F32).ap()
    permb = alloc("perm_sb", [128, 128], BF16).ap()
    gate_sb = alloc("gate_sb", [128, KC * 9], BF16).ap()
    rout_s = alloc("rout_s", [128, TGT * 9], F32).ap()
    rout = alloc("rout", [128, TGT * 9], F32).ap()
    rt = alloc("rt", [128, 32], F32).ap()
    rtv = alloc("rtv", [128, 8], F32).ap()
    sml = alloc("sml", [128, 16], F32).ap()
    ssum = alloc("ssum", [128, 16], F32).ap()
    av_sb = alloc("av_sb", [128, 256], BF16).ap()
    scT = alloc("scT", [16, TG], F32).ap()

    ps = [nc.alloc_psum_tensor(f"ps{i}", [128, 512], F32).ap() for i in range(6)]
    psb = [nc.alloc_psum_tensor(f"psb{i}", [128, 1024], BF16).ap() for i in range(2)]

    from contextlib import ExitStack
    ctx = ExitStack()
    P = Prog(nc, ctx)

    c_ap = lambda i: consts[:, i:i + 1]

    for dst, src, tok in ((consts, consts_in, "consts"), (idf, idf_in, "idf"),
                          (idb, idb_in, "idb"), (maskb, mask_in, "maskb"),
                          (permb, perm_in, "permb"), (ctbl, ctbl_in, "ctbl"),
                          (stbl, stbl_in, "stbl")):
        P.op("sync", lambda e, d=dst, s=src: e.dma_start(d[:, :], s[:, :]), writes=[tok], dma=tok)
    P.op("gpsimd", lambda e: e.dma_start(gate_sb[:, :], gateT.rearrange("(kc p) e -> p kc e", p=128)),
         writes=["gate_sb"], dma="gate_sb")

    state = {"gq": 0, "wpar": 0, "dpar": 0}

    def emit_ln_stats(ti, sl, src_ap, src_tok, with_ao):
        xb = sl
        xt = f"x{xb}"
        sb = 8 * sl
        smt = f"sml{sl}"
        pb = sl * PMAX
        ptok = f"p{sl}"
        P.op("sync", lambda e: e.dma_start(x_sb[xb][:, :], src_ap[ti * 128:(ti + 1) * 128, :]),
             reads=[src_tok] if src_tok else (), writes=[xt], dma=xt)
        if with_ao:
            gtok = ti * 128
            reg = (gtok // T) * 2 + ((gtok % T) // TH)
            P.op("sync", lambda e: e.dma_start_transpose(ao_sb[:, :], aoh[:, ti * 128:(ti + 1) * 128]),
                 reads=[f"aohr{reg}"], writes=["ao_sb"], dma="ao_sb")
            P.op("vector", lambda e: e.tensor_add(x_sb[xb][:, :], x_sb[xb][:, :], ao_sb[:, :]),
                 reads=[xt, "ao_sb"], writes=[xt])
        P.op("scalar", lambda e, pb=pb: e.activation(p_sb[:, pb:pb + D], x_sb[xb][:, :], AF.Copy,
                                                     accum_out=sml[:, sb + 1:sb + 2]),
             reads=[xt], writes=[ptok, smt])
        P.op("scalar", lambda e, pb=pb: e.activation(p_sb[:, pb:pb + D], x_sb[xb][:, :], AF.Square,
                                                     accum_out=sml[:, sb + 0:sb + 1]),
             reads=[xt], writes=[ptok, smt])
        P.op("vector", lambda e: e.tensor_scalar(sml[:, sb + 3:sb + 4], sml[:, sb + 1:sb + 2],
                                                 sml[:, sb + 1:sb + 2], c_ap(CN_NEGRECD),
                                                 ALU.mult, ALU.mult),
             reads=[smt, "consts"], writes=[smt])
        P.op("vector", lambda e: e.tensor_scalar(sml[:, sb + 2:sb + 3], sml[:, sb + 1:sb + 2],
                                                 c_ap(CN_NEGRECD), None, ALU.mult),
             reads=[smt, "consts"], writes=[smt])
        P.op("vector", lambda e: e.tensor_add(sml[:, sb + 4:sb + 5], sml[:, sb + 3:sb + 4],
                                              sml[:, sb + 0:sb + 1]),
             reads=[smt], writes=[smt])
        P.op("scalar", lambda e: e.activation(sml[:, sb + 5:sb + 6], sml[:, sb + 4:sb + 5],
                                              AF.Sqrt, scale=c_ap(CN_RECD), bias=c_ap(CN_EPS)),
             reads=[smt, "consts"], writes=[smt])
        P.op("vector", lambda e: e.reciprocal(sml[:, sb + 6:sb + 7], sml[:, sb + 5:sb + 6]),
             reads=[smt], writes=[smt])
        P.op("vector", lambda e, pb=pb: e.tensor_scalar(p_sb[:, pb:pb + D], x_sb[xb][:, :],
                                                        sml[:, sb + 2:sb + 3], sml[:, sb + 6:sb + 7],
                                                        ALU.add, ALU.mult),
             reads=[xt, smt], writes=[ptok])

    def emit_ln_transpose(sl, xlnT_ap, xlnT_tok, tcol):
        pb = sl * PMAX
        ptok = f"p{sl}"
        fns = [lambda e, kc=kc, pb=pb: e.transpose(
            psb[kc // 8][:, (kc % 8) * 128:(kc % 8 + 1) * 128],
            p_sb[:, pb + kc * 128:pb + (kc + 1) * 128], idb[:, :]) for kc in range(KC)]
        P.mm_group(fns, reads=[ptok, "idb"], writes=["psb0", "psb1"] if KC > 8 else ["psb0"])
        n0 = min(KC, 8)
        d3 = xlnT_ap.rearrange("q (kc t) -> q kc t", kc=KC)[:, 0:n0, tcol:tcol + 128]
        s3 = psb[0].rearrange("q (kc t) -> q kc t", kc=8)[:, 0:n0, :]
        P.op("scalar", lambda e, d3=d3, s3=s3: e.copy(d3, s3), reads=["psb0"], writes=[xlnT_tok])
        if KC > 8:
            d3b = xlnT_ap.rearrange("q (kc t) -> q kc t", kc=KC)[:, 8:KC, tcol:tcol + 128]
            s3b = psb[1].rearrange("q (kc t) -> q kc t", kc=8)[:, 0:KC - 8, :]
            P.op("scalar", lambda e, d3b=d3b, s3b=s3b: e.copy(d3b, s3b), reads=["psb1"], writes=[xlnT_tok])

    def ln_group_steps(tis, src_ap, src_tok_fn, xlnT_ap, xlnT_tok, tcols, with_ao):
        prev = None
        for i, ti in enumerate(tis):
            sl = i % 2
            emit_ln_stats(ti, sl, src_ap, src_tok_fn(ti), with_ao)
            if prev is not None:
                emit_ln_transpose(prev[1], xlnT_ap, xlnT_tok, prev[0])
            prev = (tcols[i], sl)
            yield
        emit_ln_transpose(prev[1], xlnT_ap, xlnT_tok, prev[0])
        yield

    def emit_ln_group(tis, src_ap, src_tok_fn, xlnT_ap, xlnT_tok, tcols, with_ao):
        for _ in ln_group_steps(tis, src_ap, src_tok_fn, xlnT_ap, xlnT_tok, tcols, with_ao):
            pass

    def emit_ffn(src_ap, src_tok_fn, dst_ap, dst_tok_fn, with_ao):
        P.switch_region(["xlnT_a"], [f"acc{t}" for t in range(TGT)])
        P.switch_region(["aoT"], ["xlnT_f"])
        P.switch_region(["qrotT", "krotT", "v_sb", "q_sb", "k_sb", "wo_out0", "wo_out1"],
                        [f"h{m}" for m in range(MH)] + ["wd0", "wd1"])
        P.switch_region(["qkvw0", "qkvw1", "wom0", "wom1"], ["wgu0", "wgu1"])
        emit_ln_group([t for t in range(TGT)], src_ap, src_tok_fn, xlnT_f, "xlnT_f",
                      [t * 128 for t in range(TGT)], with_ao)
        for g in range(NG):
            t0 = g * TGT
            for noff, w in chunks(TG):
                fns = [lambda e, kc=kc, noff=noff, w=w: e.matmul(
                    ps[0][0:9, 0:w], gate_sb[:, kc * 9:(kc + 1) * 9],
                    xlnT_f[:, kc * TG + noff:kc * TG + noff + w],
                    start=(kc == 0), stop=(kc == KC - 1)) for kc in range(KC)]
                P.mm_group(fns, reads=["gate_sb", "xlnT_f"], writes=["ps0"])
                P.op("scalar", lambda e, noff=noff, w=w: e.copy(
                    scT[0:9, noff:noff + w], ps[0][0:9, 0:w]), reads=["ps0"], writes=["scT"])
            for t in range(TGT):
                P.mm_group([lambda e, t=t: e.transpose(ps[1][:, 0:9],
                                                       scT[0:9, t * 128:(t + 1) * 128],
                                                       idf[0:9, 0:9])],
                           reads=["scT", "idf"], writes=["ps1"])
                P.op("scalar", lambda e, t=t: e.copy(rout_s[:, t * 9:(t + 1) * 9], ps[1][:, 0:9]),
                     reads=["ps1"], writes=["rout_s"])
            for t in range(TGT):
                s8 = rout_s[:, t * 9:t * 9 + 8]
                P.op("vector", lambda e, s8=s8: e.reduce_max(rtv[:, 0:1], s8, axis=AX.X),
                     reads=["rout_s"], writes=["rtv"])
                P.op("vector", lambda e, s8=s8: e.tensor_scalar(rt[:, 0:8], s8, rtv[:, 0:1], None, ALU.is_ge),
                     reads=["rout_s", "rtv"], writes=["rt"])
                P.op("vector", lambda e, s8=s8: e.scalar_tensor_tensor(rt[:, 8:16], rt[:, 0:8],
                                                                       -1e30, s8, ALU.mult, ALU.add),
                     reads=["rt", "rout_s"], writes=["rt"])
                P.op("vector", lambda e: e.reduce_max(rtv[:, 1:2], rt[:, 8:16], axis=AX.X),
                     reads=["rt"], writes=["rtv"])
                P.op("vector", lambda e: e.tensor_sub(rtv[:, 2:3], rtv[:, 0:1], rtv[:, 1:2]),
                     reads=["rtv"], writes=["rtv"])
                P.op("scalar", lambda e: e.activation(rtv[:, 3:4], rtv[:, 2:3], AF.Sigmoid),
                     reads=["rtv"], writes=["rtv"])
                P.op("scalar", lambda e: e.activation(rtv[:, 4:5], rtv[:, 2:3], AF.Sigmoid,
                                                      scale=c_ap(CN_NEG1)),
                     reads=["rtv", "consts"], writes=["rtv"])
                P.op("vector", lambda e: e.tensor_scalar(rt[:, 16:24], rt[:, 8:16], rtv[:, 1:2],
                                                         None, ALU.is_ge),
                     reads=["rt", "rtv"], writes=["rt"])
                P.op("vector", lambda e: e.tensor_scalar(rt[:, 24:32], rt[:, 16:24], rtv[:, 4:5],
                                                         None, ALU.mult),
                     reads=["rt", "rtv"], writes=["rt"])
                P.op("vector", lambda e, t=t: e.scalar_tensor_tensor(rout[:, t * 9:t * 9 + 8],
                                                                     rt[:, 0:8], rtv[:, 3:4],
                                                                     rt[:, 24:32], ALU.mult, ALU.add),
                     reads=["rt", "rtv"], writes=["rout"])
                P.op("scalar", lambda e, t=t: e.activation(rout[:, t * 9 + 8:t * 9 + 9],
                                                           rout_s[:, t * 9 + 8:t * 9 + 9], AF.Sigmoid),
                     reads=["rout_s"], writes=["rout"])

            def a_phase(wsrc, mh, chunk0):
                for m in range(mh):
                    wp = state["wpar"] % 2
                    state["wpar"] += 1
                    gb = wgu_b[wp]
                    gt = f"wgu{wp}"
                    r0 = (chunk0 + m) * 128
                    P.op("gpsimd", lambda e, gb=gb, r0=r0, wsrc=wsrc: e.dma_start(
                        gb[:, :], wsrc[r0:r0 + 128, :]), writes=[gt], dma=gt)
                    for ci, (noff, w) in enumerate(chunks(TG)):
                        par = (m * len(chunks(TG)) + ci) % 3
                        pg, pu = 2 * par, 2 * par + 1
                        fns = [lambda e, kc=kc, pg=pg, gb=gb, noff=noff, w=w: e.matmul(
                            ps[pg][:, 0:w], gb[:, kc * 128:(kc + 1) * 128],
                            xlnT_f[:, kc * TG + noff:kc * TG + noff + w],
                            start=(kc == 0), stop=(kc == KC - 1)) for kc in range(KC)]
                        P.mm_group(fns, reads=[gt, "xlnT_f"], writes=[f"ps{pg}"])
                        fns = [lambda e, kc=kc, pu=pu, gb=gb, noff=noff, w=w: e.matmul(
                            ps[pu][:, 0:w], gb[:, (KC + kc) * 128:(KC + kc + 1) * 128],
                            xlnT_f[:, kc * TG + noff:kc * TG + noff + w],
                            start=(kc == 0), stop=(kc == KC - 1)) for kc in range(KC)]
                        P.mm_group(fns, reads=[gt, "xlnT_f"], writes=[f"ps{pu}"])
                        sgp = par % 2
                        P.op("scalar", lambda e, sgp=sgp, pg=pg, w=w: e.activation(
                            sg_sb[sgp][:, 0:w], ps[pg][:, 0:w], AF.Silu),
                            reads=[f"ps{pg}"], writes=[f"sg{sgp}"])
                        P.op("vector", lambda e, sgp=sgp, pu=pu, m=m, noff=noff, w=w: e.tensor_mul(
                            h_buf[:, m * TG + noff:m * TG + noff + w], sg_sb[sgp][:, 0:w], ps[pu][:, 0:w]),
                            reads=[f"sg{sgp}", f"ps{pu}"], writes=[f"h{m}"])

            def b_phase(wsrc_d, mh, chunk0, scol_fn, first_write, filler=None):
                for cq_i, (cqoff, w) in enumerate(chunks(D)):
                    if filler is not None:
                        filler()
                    dp = state["dpar"] % 2
                    state["dpar"] += 1
                    db, dt_ = wd_b[dp], f"wd{dp}"
                    r0 = (chunk0 + cq_i) * 128
                    P.op("gpsimd", lambda e, db=db, r0=r0, mh=mh, w=w, wsrc_d=wsrc_d: e.dma_start(
                        db[:, 0:mh * w], wsrc_d[r0:r0 + 128, 0:mh * w]), writes=[dt_], dma=dt_)
                    for t in range(TGT):
                        pp = 2 + ((cq_i * TGT + t) % 4)
                        fns = [lambda e, m=m, pp=pp, db=db, t=t, w=w: e.matmul(
                            ps[pp][:, 0:w], h_buf[:, m * TG + t * 128:m * TG + (t + 1) * 128],
                            db[:, m * w:(m + 1) * w],
                            start=(m == 0), stop=(m == mh - 1)) for m in range(mh)]
                        P.mm_group(fns, reads=[f"h{m}" for m in range(mh)] + [dt_], writes=[f"ps{pp}"])
                        scol = scol_fn(t)
                        a_sl = acc[:, t * D + cqoff:t * D + cqoff + w]
                        if first_write:
                            P.op("vector", lambda e, pp=pp, scol=scol, a_sl=a_sl, w=w: e.tensor_scalar(
                                a_sl, ps[pp][:, 0:w], scol, None, ALU.mult),
                                reads=[f"ps{pp}", "rout"], writes=[f"acc{t}"])
                        else:
                            P.op("vector", lambda e, pp=pp, scol=scol, a_sl=a_sl, w=w: e.scalar_tensor_tensor(
                                a_sl, ps[pp][:, 0:w], scol, a_sl, ALU.mult, ALU.add),
                                reads=[f"ps{pp}", "rout", f"acc{t}"], writes=[f"acc{t}"])

            for ex in range(NE):
                a_phase(wgu, MH, ex * MH)
                b_phase(wdc, MH, ex * NCC, lambda t, ex=ex: rout[:, t * 9 + ex:t * 9 + ex + 1],
                        first_write=(ex == 0))
            ln_gen = None
            if g + 1 < NG:
                nt0 = (g + 1) * TGT
                ln_gen = ln_group_steps([nt0 + t for t in range(TGT)], src_ap, src_tok_fn,
                                        xlnT_f, "xlnT_f", [t * 128 for t in range(TGT)], with_ao)

            def filler(k=(2 * TGT + 2) // max(1, 2 * NCC) + 1):
                if ln_gen is not None:
                    for _ in range(k):
                        if next(ln_gen, "end") == "end":
                            break
            for hf in range(2):
                a_phase(shgu, MSH, hf * MSH)
                b_phase(shdc, MSH, hf * NCC, lambda t: rout[:, t * 9 + 8:t * 9 + 9],
                        first_write=False, filler=filler if hf == 1 else None)
            if ln_gen is not None:
                for _ in ln_gen:
                    pass

            for t in range(TGT):
                ti = t0 + t
                xb = t % 2
                xt = f"x{xb}"
                P.op("sync", lambda e, ti=ti, xb=xb: e.dma_start(
                    x_sb[xb][:, :], src_ap[ti * 128:(ti + 1) * 128, :]),
                    reads=[src_tok_fn(ti)] if src_tok_fn(ti) else (), writes=[xt], dma=xt)
                if with_ao:
                    reg = ((ti * 128) // T) * 2 + (((ti * 128) % T) // TH)
                    P.op("sync", lambda e, ti=ti: e.dma_start_transpose(
                        ao_sb[:, :], aoh[:, ti * 128:(ti + 1) * 128]),
                        reads=[f"aohr{reg}"], writes=["ao_sb"], dma="ao_sb")
                P.op("vector", lambda e, t=t, xb=xb: e.scalar_tensor_tensor(
                    x_sb[xb][:, :], acc[:, t * D:(t + 1) * D], c_ap(CN_HALF), x_sb[xb][:, :],
                    ALU.mult, ALU.add),
                    reads=[f"acc{t}", "consts", xt], writes=[xt])
                if with_ao:
                    P.op("vector", lambda e, xb=xb: e.tensor_add(x_sb[xb][:, :], x_sb[xb][:, :], ao_sb[:, :]),
                         reads=[xt, "ao_sb"], writes=[xt])
                P.op("sync", lambda e, ti=ti, xb=xb: e.dma_start(
                    dst_ap[ti * 128:(ti + 1) * 128, :], x_sb[xb][:, :]),
                    reads=[xt], writes=[dst_tok_fn(ti)], dma=xt)

    def emit_attn():
        P.switch_region([f"acc{t}" for t in range(TGT)], ["xlnT_a"])
        P.switch_region(["xlnT_f"], ["aoT"])
        P.switch_region([f"h{m}" for m in range(MH)] + ["wd0", "wd1"],
                        ["qrotT", "krotT", "v_sb", "q_sb", "k_sb", "wo_out0", "wo_out1"])
        P.switch_region(["wgu0", "wgu1"], ["qkvw0", "qkvw1", "wom0", "wom1"])
        pcyc = [0]

        def nps():
            i = pcyc[0] % 4
            pcyc[0] += 1
            return i

        emit_ln_group([tl for tl in range(T // 128)], xres, lambda ti: f"xr{ti}",
                      xlnT_a, "xlnT_a", [tl * 128 for tl in range(T // 128)], False)
        for b in range(Bc):
            bt0 = b * (T // 128)
            for hf in range(2):
                qoff0 = hf * TH
                kvlen = (hf + 1) * TH
                for h in range(NH):
                    plan = [(0, TH, qoff0, q_sb, "q_sb"), (1, kvlen, 0, k_sb, "k_sb")]
                    pchunks = {}
                    for which, length, poff, stage, stok in plan:
                        wp = state["wpar"] % 2
                        state["wpar"] += 1
                        wb, wt = qkvw_b[wp], f"qkvw{wp}"
                        rows = (h * 3 + which) * 128
                        P.op("gpsimd", lambda e, wb=wb, rows=rows: e.dma_start(
                            wb[:, :], wqkvh[rows:rows + 128, :]), writes=[wt], dma=wt)
                        lst = []
                        for noff, w in chunks(length):
                            pi = nps()
                            fns = [lambda e, kc=kc, pi=pi, wb=wb, noff=noff, w=w, poff=poff: e.matmul(
                                ps[pi][:, 0:w], wb[:, kc * 128:(kc + 1) * 128],
                                xlnT_a[:, kc * T + poff + noff:kc * T + poff + noff + w],
                                start=(kc == 0), stop=(kc == KC - 1)) for kc in range(KC)]
                            P.mm_group(fns, reads=[wt, "xlnT_a"], writes=[f"ps{pi}"])
                            P.op("scalar", lambda e, pi=pi, noff=noff, w=w, sb=stage: e.copy(
                                sb[:, noff:noff + w], ps[pi][:, 0:w]), reads=[f"ps{pi}"], writes=[stok])
                            lst.append((noff, w, pi))
                        pchunks[which] = lst
                    for which, dst, dst_tok, stage, stok, poff in (
                            (0, qrotT, "qrotT", q_sb, "q_sb", qoff0), (1, krotT, "krotT", k_sb, "k_sb", 0)):
                        for noff, w, _ in pchunks[which]:
                            pi = nps()
                            P.mm_group([lambda e, pi=pi, noff=noff, w=w, sb=stage: e.matmul(
                                ps[pi][:, 0:w], permb[:, :], sb[:, noff:noff + w], start=True, stop=True)],
                                reads=["permb", stok], writes=[f"ps{pi}"])
                            P.op("vector", lambda e, noff=noff, w=w, sb=stage, poff=poff: e.tensor_mul(
                                rsc[:, 0:w], sb[:, noff:noff + w], ctbl[:, poff + noff:poff + noff + w]),
                                reads=[stok, "ctbl"], writes=["rsc0"])
                            P.op("vector", lambda e, pi=pi, noff=noff, w=w, poff=poff: e.tensor_mul(
                                rsc[:, 512:512 + w], ps[pi][:, 0:w], stbl[:, poff + noff:poff + noff + w]),
                                reads=[f"ps{pi}", "stbl"], writes=["rsc1"])
                            P.op("vector", lambda e, noff=noff, w=w, dst=dst: e.tensor_add(
                                dst[:, noff:noff + w], rsc[:, 0:w], rsc[:, 512:512 + w]),
                                reads=["rsc0", "rsc1"], writes=[dst_tok])
                    wp = state["wpar"] % 2
                    state["wpar"] += 1
                    wb, wt = qkvw_b[wp], f"qkvw{wp}"
                    rows = (h * 3 + 2) * 128
                    P.op("gpsimd", lambda e, wb=wb, rows=rows: e.dma_start(
                        wb[:, :], wqkvh[rows:rows + 128, :]), writes=[wt], dma=wt)
                    for noff, w in chunks(kvlen):
                        pi = nps()
                        fns = [lambda e, kc=kc, pi=pi, wb=wb, noff=noff, w=w: e.matmul(
                            ps[pi][:, 0:w], wb[:, kc * 128:(kc + 1) * 128],
                            xlnT_a[:, kc * T + noff:kc * T + noff + w],
                            start=(kc == 0), stop=(kc == KC - 1)) for kc in range(KC)]
                        P.mm_group(fns, reads=[wt, "xlnT_a"], writes=[f"ps{pi}"])
                        P.op("scalar", lambda e, pi=pi, noff=noff, w=w: e.copy(
                            pt_sb[:, noff:noff + w], ps[pi][:, 0:w]), reads=[f"ps{pi}"], writes=["pt"])
                    nvc = kvlen // 128
                    fns = [lambda e, kc=kc: e.transpose(
                        psb[kc // 8][:, (kc % 8) * 128:(kc % 8 + 1) * 128],
                        pt_sb[:, kc * 128:(kc + 1) * 128], idb[:, :]) for kc in range(nvc)]
                    P.mm_group(fns, reads=["pt", "idb"],
                               writes=["psb0", "psb1"] if nvc > 8 else ["psb0"])
                    n0 = min(nvc, 8)
                    P.op("scalar", lambda e, n0=n0: e.copy(v_sb[:, 0:n0 * 128], psb[0][:, 0:n0 * 128]),
                         reads=["psb0"], writes=["v_sb"])
                    if nvc > 8:
                        n1 = nvc - 8
                        P.op("scalar", lambda e, n1=n1: e.copy(
                            v_sb[:, 1024:1024 + n1 * 128], psb[1][:, 0:n1 * 128]),
                            reads=["psb1"], writes=["v_sb"])

                    def s1bank(ql, ci):
                        return (ci + 2 * (ql % 2)) % 4

                    def stage1mm(ql):
                        qi = hf * QTH + ql
                        L = (qi + 1) * 128
                        scs = chunks(L)
                        for ci, (noff, w) in enumerate(scs):
                            pc = s1bank(ql, ci)
                            P.mm_group([lambda e, pc=pc, noff=noff, w=w, ql=ql: e.matmul(
                                ps[pc][:, 0:w], qrotT[:, ql * 128:(ql + 1) * 128],
                                krotT[:, noff:noff + w], start=True, stop=True)],
                                reads=["qrotT", "krotT"], writes=[f"ps{pc}"])

                    def stage1post(ql):
                        qi = hf * QTH + ql
                        L = (qi + 1) * 128
                        scs = chunks(L)
                        sl = ql % 2
                        pb = sl * PMAX
                        sb8 = sl * 8
                        dci, doff = s1bank(ql, (L - 128) // 512), (L - 128) % 512
                        P.op("vector", lambda e, dci=dci, doff=doff: e.tensor_add(
                            ps[dci][:, doff:doff + 128], ps[dci][:, doff:doff + 128], maskb[:, :]),
                            reads=[f"ps{dci}", "maskb"], writes=[f"ps{dci}"])
                        order = sorted(range(len(scs)), key=lambda ci: (ci - 2) % 4)
                        for ci in order:
                            noff, w = scs[ci]
                            pc = s1bank(ql, ci)
                            P.op("scalar", lambda e, pc=pc, ci=ci, noff=noff, w=w, pb=pb, sb8=sb8: e.activation(
                                p_sb[:, pb + noff:pb + noff + w], ps[pc][:, 0:w], AF.Exp,
                                scale=c_ap(CN_ISQH), bias=c_ap(CN_NEGSHIFT),
                                accum_out=ssum[:, sb8 + ci:sb8 + ci + 1]),
                                reads=[f"ps{pc}", "consts"], writes=[f"p{sl}", "ssum"])
                        ncs = len(scs)
                        if ncs > 1:
                            P.op("vector", lambda e, ncs=ncs, sb8=sb8: e.reduce_sum(
                                ssum[:, sb8 + 4:sb8 + 5], ssum[:, sb8:sb8 + ncs], axis=AX.X),
                                reads=["ssum"], writes=["ssum"])
                            scol = 4
                        else:
                            scol = 0
                        P.op("vector", lambda e, scol=scol, sb8=sb8: e.reciprocal(
                            ssum[:, sb8 + 5:sb8 + 6], ssum[:, sb8 + scol:sb8 + scol + 1]),
                            reads=["ssum"], writes=["ssum"])

                    def _halves(ql):
                        qi = hf * QTH + ql
                        L = (qi + 1) * 128
                        nkc = L // 128
                        ha = max(1, min(nkc, 8) if nkc > 8 else (nkc + 1) // 2)
                        hv = [(0, ha, "pt_a"), (ha, nkc, "pt_b")] if ha < nkc else [(0, nkc, "pt_a")]
                        return nkc, hv

                    def stage2a(ql):
                        nkc, halves = _halves(ql)
                        sl = ql % 2
                        pb = sl * PMAX
                        for hi_, (k0, k1, ptok_) in enumerate(halves):
                            fns = [lambda e, kc=kc, pb=pb: e.transpose(
                                psb[kc // 8][:, (kc % 8) * 128:(kc % 8 + 1) * 128],
                                p_sb[:, pb + kc * 128:pb + (kc + 1) * 128], idb[:, :])
                                for kc in range(k0, k1)]
                            wr = sorted({f"psb{kc // 8}" for kc in range(k0, k1)})
                            P.mm_group(fns, reads=[f"p{sl}", "idb"], writes=wr)
                            if k1 <= 8 or k0 >= 8:
                                bsel = 0 if k1 <= 8 else 1
                                o0 = k0 - 8 * bsel
                                o1 = k1 - 8 * bsel
                                P.op("vector", lambda e, bsel=bsel, o0=o0, o1=o1, k0=k0: e.tensor_copy(
                                    pt_sb[:, k0 * 128:(k0 + (o1 - o0)) * 128],
                                    psb[bsel][:, o0 * 128:o1 * 128]),
                                    reads=[f"psb{bsel}"], writes=[ptok_, "pt"] if hi_ == 0 else [ptok_])
                            else:
                                P.op("vector", lambda e, k0=k0: e.tensor_copy(
                                    pt_sb[:, k0 * 128:1024], psb[0][:, k0 * 128:1024]),
                                    reads=["psb0"], writes=[ptok_, "pt"] if hi_ == 0 else [ptok_])
                                P.op("vector", lambda e, k1=k1: e.tensor_copy(
                                    pt_sb[:, 1024:k1 * 128], psb[1][:, 0:(k1 - 8) * 128]),
                                    reads=["psb1"], writes=[ptok_])
                    def stage2b(ql):
                        nkc, halves = _halves(ql)
                        sl = ql % 2
                        sb8 = sl * 8
                        pav = 4 + sl
                        for hi, (k0, k1, ptok_) in enumerate(halves):
                            fns = [lambda e, kc=kc, pav=pav, k0=k0, hi=hi: e.matmul(
                                ps[pav][:, 0:128], pt_sb[:, kc * 128:(kc + 1) * 128],
                                v_sb[:, kc * 128:(kc + 1) * 128],
                                start=(kc == k0 and hi == 0), stop=(kc == nkc - 1))
                                for kc in range(k0, k1)]
                            rds = [ptok_, "v_sb"] + (["pt"] if hi == len(halves) - 1 else [])
                            P.mm_group(fns, reads=rds, writes=[f"ps{pav}"])
                        P.op("scalar", lambda e, pav=pav, sl=sl, sb8=sb8: e.activation(
                            av_sb[:, sl * 128:(sl + 1) * 128], ps[pav][:, 0:128],
                            AF.Copy, scale=ssum[:, sb8 + 5:sb8 + 6]),
                            reads=[f"ps{pav}", "ssum"], writes=[f"av{sl}"])
                    def stage2c(ql):
                        sl = ql % 2
                        P.mm_group([lambda e, sl=sl: e.transpose(
                            psb[0][:, 0:128], av_sb[:, sl * 128:(sl + 1) * 128], idb[:, :])],
                            reads=[f"av{sl}", "idb"], writes=["psb0"])
                        P.op("scalar", lambda e, h=h, ql=ql: e.copy(
                            aoT[:, h * TH + ql * 128:h * TH + (ql + 1) * 128], psb[0][:, 0:128]),
                            reads=["psb0"], writes=["aoT"])

                    prev = None
                    pend_c = None
                    for ql in range(QTH):
                        stage1mm(ql)
                        if pend_c is not None:
                            stage2c(pend_c)
                            pend_c = None
                        if prev is not None:
                            stage2a(prev)
                            stage2b(prev)
                            pend_c = prev
                        stage1post(ql)
                        prev = ql
                    if pend_c is not None:
                        stage2c(pend_c)
                    stage2a(prev)
                    stage2b(prev)
                    stage2c(prev)
                ln_gen_a = None
                if hf == 1 and b + 1 < Bc:
                    nb0 = (b + 1) * (T // 128)
                    ln_gen_a = ln_group_steps(
                        [nb0 + tl for tl in range(T // 128)], xres, lambda ti: f"xr{ti}",
                        xlnT_a, "xlnT_a", [tl * 128 for tl in range(T // 128)], False)
                for m in range(KC):
                    if ln_gen_a is not None:
                        for _ in range(2):
                            if next(ln_gen_a, "end") == "end":
                                ln_gen_a = None
                                break
                    wp = state["wpar"] % 2
                    state["wpar"] += 1
                    wb, wt = wom_b[wp], f"wom{wp}"
                    P.op("gpsimd", lambda e, wb=wb, m=m: e.dma_start(
                        wb[:, :], wo_in[m * 128:(m + 1) * 128, :]), writes=[wt], dma=wt)
                    op_ = m % 2
                    wot = f"wo_out{op_}"
                    for ci, (noff, w) in enumerate(chunks(TH)):
                        pw = 2 * op_ + (ci % 2)
                        fns = [lambda e, cp=cp, pw=pw, wb=wb, noff=noff, w=w: e.matmul(
                            ps[pw][:, 0:w], wb[:, cp * 128:(cp + 1) * 128],
                            aoT[:, cp * TH + noff:cp * TH + noff + w],
                            start=(cp == 0), stop=(cp == KC - 1)) for cp in range(KC)]
                        P.mm_group(fns, reads=[wt, "aoT"], writes=[f"ps{pw}"])
                        P.op("scalar", lambda e, pw=pw, noff=noff, w=w, op_=op_: e.copy(
                            wo_out[op_][:, noff:noff + w], ps[pw][:, 0:w]),
                            reads=[f"ps{pw}"], writes=[wot])
                    P.op("sync", lambda e, m=m, b=b, qoff0=qoff0, op_=op_: e.dma_start(
                        aoh[m * 128:(m + 1) * 128, b * T + qoff0:b * T + qoff0 + TH], wo_out[op_][:, :]),
                        reads=[wot], writes=[f"aohr{b * 2 + hf}"], dma=wot)
                if ln_gen_a is not None:
                    for _ in ln_gen_a:
                        pass

    state["head"] = 0
    emit_ffn(x_in, lambda ti: None, xres, lambda ti: f"xr{ti}", with_ao=False)
    emit_attn()
    emit_ffn(xres, lambda ti: f"xr{ti}", out_t, lambda ti: f"out{ti}", with_ao=True)
    P.finalize()
    ctx.close()
    return nc


def host_prep(cfg, L0_attn_w_qkv, L0_attn_w_o, L0_ffn_gate_w, L0_ffn_w_gate, L0_ffn_w_up,
              L0_ffn_w_down, L0_ffn_sh_gate, L0_ffn_sh_up, L0_ffn_sh_down, L0_ffn_se_gate_w):
    Bc, T, D, NH, NE, H, HS = (cfg[k] for k in ("Bc", "T", "D", "NH", "NE", "H", "HS"))
    KC = D // 128
    MH = H // 128
    HS2 = HS // 2
    MSH = HS2 // 128
    CW = min(512, D)
    NCC = D // CW
    f32 = np.float32
    bfl = ml_dtypes.bfloat16
    w = {}
    w["wqkvh"] = np.ascontiguousarray(
        np.asarray(L0_attn_w_qkv, f32).reshape(KC, 128, 3, NH, 128)
        .transpose(3, 2, 1, 0, 4).reshape(NH * 3 * 128, KC * 128))
    w["wo"] = np.ascontiguousarray(
        np.asarray(L0_attn_w_o, f32).reshape(KC, 128, KC, 128)
        .transpose(2, 1, 0, 3).reshape(KC * 128, KC * 128))
    w["gateT"] = np.ascontiguousarray(
        np.concatenate([np.asarray(L0_ffn_gate_w, f32), np.asarray(L0_ffn_se_gate_w, f32)], axis=0).T)
    wg5 = np.asarray(L0_ffn_w_gate, f32).reshape(NE, MH, 128, KC, 128).transpose(0, 1, 4, 3, 2)
    wu5 = np.asarray(L0_ffn_w_up, f32).reshape(NE, MH, 128, KC, 128).transpose(0, 1, 4, 3, 2)
    w["wgu"] = np.ascontiguousarray(np.concatenate(
        [wg5.reshape(NE, MH, 128, KC * 128), wu5.reshape(NE, MH, 128, KC * 128)],
        axis=3).reshape(NE * MH * 128, 2 * KC * 128))
    w["wdc"] = np.ascontiguousarray(
        np.asarray(L0_ffn_w_down, f32).reshape(NE, NCC, CW, MH, 128)
        .transpose(0, 1, 4, 3, 2).reshape(NE * NCC * 128, MH * CW))
    sg5 = np.asarray(L0_ffn_sh_gate, f32).reshape(2, MSH, 128, KC, 128).transpose(0, 1, 4, 3, 2)
    su5 = np.asarray(L0_ffn_sh_up, f32).reshape(2, MSH, 128, KC, 128).transpose(0, 1, 4, 3, 2)
    w["shgu"] = np.ascontiguousarray(np.concatenate(
        [sg5.reshape(2, MSH, 128, KC * 128), su5.reshape(2, MSH, 128, KC * 128)],
        axis=3).reshape(2 * MSH * 128, 2 * KC * 128))
    w["shdc"] = np.ascontiguousarray(
        np.asarray(L0_ffn_sh_down, f32).T.reshape(2, MSH, 128, NCC, CW)
        .transpose(0, 3, 2, 1, 4).reshape(2 * NCC * 128, MSH * CW))
    c = np.zeros((128, NCONST), f32)
    c[:, CN_NEGRECD] = -1.0 / D
    c[:, CN_EPS] = 1e-5
    c[:, CN_RECD] = 1.0 / D
    c[:, CN_ISQH] = 1.0 / math.sqrt(128.0)
    c[:, CN_NEGSHIFT] = -EXP_SHIFT
    c[:, CN_HALF] = 0.5
    c[:, CN_NEG1] = -1.0
    w["consts"] = c
    w["idf"] = np.eye(128, dtype=f32)
    w["idb"] = np.eye(128, dtype=f32).astype(bfl)
    m = np.zeros((128, 128), f32)
    m[np.triu_indices(128, k=1)] = -1e30
    w["maskc"] = m
    pm = np.zeros((128, 128), f32)
    pm[(np.arange(128) + 64) % 128, np.arange(128)] = 1.0
    w["permb"] = pm.astype(bfl)
    inv = (1.0 / (10000.0 ** (np.arange(0, 128, 2, dtype=f32) / f32(128.0)))).astype(f32)
    t = np.arange(T, dtype=f32)
    fr = np.outer(t, inv).astype(f32)
    ct = np.zeros((128, T), f32)
    st = np.zeros((128, T), f32)
    ct[0:64] = np.cos(fr).T
    ct[64:128] = np.cos(fr).T
    st[0:64] = -np.sin(fr).T
    st[64:128] = np.sin(fr).T
    w["ctbl"] = ct
    w["stbl"] = st
    return w


_NC_CACHE = {}


def kernel(L0_attn_x, L0_attn_w_qkv, L0_attn_w_o, L0_ffn_gate_w, L0_ffn_w_gate, L0_ffn_w_up,
           L0_ffn_w_down, L0_ffn_sh_gate, L0_ffn_sh_up, L0_ffn_sh_down, L0_ffn_se_gate_w):
    cfg = REAL
    Bc, T, D = cfg["Bc"], cfg["T"], cfg["D"]
    n_cores = 8
    x = np.asarray(L0_attn_x, np.float32)
    B = x.shape[0]
    assert B == n_cores * Bc
    w = host_prep(cfg, L0_attn_w_qkv, L0_attn_w_o, L0_ffn_gate_w, L0_ffn_w_gate, L0_ffn_w_up,
                  L0_ffn_w_down, L0_ffn_sh_gate, L0_ffn_sh_up, L0_ffn_sh_down, L0_ffn_se_gate_w)
    key = tuple(sorted(cfg.items()))
    if key not in _NC_CACHE:
        _NC_CACHE[key] = build_nc(cfg)
    nc = _NC_CACHE[key]
    in_maps = []
    for c in range(n_cores):
        m = dict(w)
        m["x"] = np.ascontiguousarray(x[c * Bc:(c + 1) * Bc].reshape(Bc * T, D))
        in_maps.append(m)
    res = run_bass_kernel_spmd(nc, in_maps, core_ids=list(range(n_cores)))
    outs = [np.asarray(res.results[c]["out"], np.float32).reshape(Bc, T, D) for c in range(n_cores)]
    return np.concatenate(outs, axis=0)
